# version 06
# baseline (speedup 1.0000x reference)
import numpy as np
import ml_dtypes
from contextlib import ExitStack
import concourse.bass as bass
import concourse.mybir as mybir
from concourse.bass_utils import run_bass_kernel_spmd

F32 = mybir.dt.float32
BF16 = mybir.dt.bfloat16
AF = mybir.ActivationFunctionType
ALU = mybir.AluOpType
AX = mybir.AxisListType

P = 128
D = 2048
T = 2048
TT = T // P
DC = D // P
NHA, HDA = 16, 128
NHR, HDR = 8, 256
NE, EI = 16, 1408
IC = EI // P
SI = 2816
JC = SI // P
TH = T // 2
TTH = TH // P
EPS_LN = 1e-5
EPS_RMS = 1e-6
EPS_GN = 1e-5
RSQ_HDA = float(1.0 / np.sqrt(HDA))

ENGS = ("tensor", "vector", "scalar", "gpsimd", "sync")
RACE_DETECT = False
RET_HEADS = NHR
RET_WO2 = True
RET_LVL = 9

HT = 0
A_QF, A_KF, A_VF, A_VT = 32768, 34816, 36864, 38912
A_PR, A_PT = 40960, 45056
A_WSL = 49152
W_H2T = 0
W_WOCH = 16384
W_AOTT = 49152
M_H2T = 0
M_EDT = 16384
M_WPC = 38912
M_GS = 43008
M_HH = 44032
M_XP = 55296
R_HT2 = 0
R_RW = 32768
R_QF, R_KF, R_VT, R_GR = 40960, 45056, 49152, 53248
R_SIMT = 57344
R_BIAS = 59392
G_BO = 53248
G_GT = 0
G_WOCH = 16384
G_SRC = 49152


def host_consts():
    c = {}
    c["ident"] = np.eye(P).astype(ml_dtypes.bfloat16)
    iq = np.arange(P)
    c["maskS"] = np.where(iq[None, :] <= iq[:, None], 0.0, -1e9).astype(np.float32)
    c["maskR"] = (iq[None, :] >= iq[:, None]).astype(np.float32)
    c["ones_bf"] = np.ones((P, P), ml_dtypes.bfloat16)
    c["identF"] = np.eye(P).astype(np.float32)
    c["epsc"] = np.broadcast_to(np.array([[1e-5, 1e-6]], np.float32), (P, 2)).copy()
    gam = 1.0 - np.exp(np.linspace(np.log(1 / 32), np.log(1 / 512), NHR))
    t = np.arange(T, dtype=np.float64)
    gq = np.zeros((P, NHR * TT), np.float32)
    gk = np.zeros((P, NHR * TT), np.float32)
    for h in range(NHR):
        for tt in range(TT):
            ts = t[tt * P:(tt + 1) * P]
            gq[:, h * TT + tt] = (gam[h] ** ts).astype(np.float32)
            gk[:, h * TT + tt] = ((gam[h] ** (-ts)) / 16.0).astype(np.float32)
    c["gq"], c["gk"] = gq, gk
    x = np.linspace(0.0, 1.0, HDR // 2)
    thetas = np.repeat(1.0 / (10000.0 ** x), 2)
    ang = t[:, None] * thetas[None, :]
    sin = np.sin(ang).astype(np.float32)
    cos = np.cos(ang).astype(np.float32)
    sin_sg = sin.copy()
    sin_sg[:, 0::2] *= -1.0
    c["sin_sg"] = np.ascontiguousarray(
        sin_sg.reshape(TT, P, HDR).transpose(1, 0, 2).reshape(P, TT * HDR))
    c["cos_t"] = np.ascontiguousarray(
        cos.reshape(TT, P, HDR).transpose(1, 0, 2).reshape(P, TT * HDR))
    return c


class Sem:
    __slots__ = ("h", "v", "name")

    def __init__(self, h, name):
        self.h, self.v, self.name = h, 0, name

    def bump(self, n=1):
        self.v += n
        return self.v


class KB:
    def __init__(self, nc, ctx):
        self.nc, self.ctx = nc, ctx
        self.ops = {e: [] for e in ENGS}
        self._sems = {}
        self.n_instr = 0
        self.last_pe = None
        self.chain_prev = {"vector": None, "scalar": None, "gpsimd": None}
        self.chain_def = {}

    def sem(self, name):
        if name not in self._sems:
            h = self.ctx.enter_context(self.nc.semaphore(name))
            self._sems[name] = Sem(h, name)
        return self._sems[name]

    def E(self, eng, meth, *args, waits=(), inc=None, incn=1, **kw):
        chain = eng in self.chain_prev and meth != "dma_start"
        if chain and inc is None:
            inc = self.chain_def[eng]
        tgt = inc.bump(incn) if inc is not None else None
        if eng == "tensor" and inc is not None:
            self.last_pe = (inc, tgt)
        if chain:
            if self.chain_prev[eng] is not None:
                waits = tuple(waits) + (self.chain_prev[eng],)
            self.chain_prev[eng] = (inc, tgt)
        wl = tuple(waits)
        for (s_, v_) in wl:
            assert isinstance(v_, int), (meth, s_.name if hasattr(s_, "name") else s_, v_)
        self.n_instr += 1 + len(wl)

        def f(e, _m=meth, _a=args, _k=kw, _wl=wl, _i=inc, _n=incn):
            for (s, v) in _wl:
                e.wait_ge(s.h, v)
            r = getattr(e, _m)(*_a, **_k)
            if _i is not None:
                r.then_inc(_i.h, _n)
        self.ops[eng].append(f)
        return tgt

    def W(self, eng, waits):
        wl = tuple(waits)
        self.n_instr += len(wl)

        def f(e, _wl=wl):
            for (s, v) in _wl:
                e.wait_ge(s.h, v)
        self.ops[eng].append(f)

    def dispatch(self, block):
        for eng in ENGS:
            lst = self.ops[eng]

            def body(e, _lst=lst):
                for f in _lst:
                    f(e)
            getattr(block, eng)(body)


class PSlot:
    def __init__(self, kb, name, shape, dtype):
        self.ps = kb.ctx.enter_context(kb.nc.psum_tensor(name, shape, dtype))
        self.done = kb.sem(name + "_d")
        self.free = kb.sem(name + "_f")
        self.fills = 0
        self.rel = None

    def begin(self):
        if self.fills > 0:
            w = [self.rel if self.rel is not None
                 else (self.free, self.free.v)]
        else:
            w = []
        self.rel = None
        self.fills += 1
        return w

    def released(self, evt):
        self.rel = evt

    def rd(self):
        return [(self.done, self.done.v)]


class Rot:
    def __init__(self, slots):
        self.slots = slots
        self.i = -1

    def next(self):
        self.i = (self.i + 1) % len(self.slots)
        return self.slots[self.i]


def evens(ap2d):
    return ap2d.rearrange("p (n two) -> p two n", two=2)[:, 0:1, :]


def odds(ap2d):
    return ap2d.rearrange("p (n two) -> p two n", two=2)[:, 1:2, :]


def build_program(upto="full"):
    nc = bass.Bass("TRN2", target_bir_lowering=False,
                   detect_race_conditions=RACE_DETECT)

    def din(name, shape, dt=F32):
        return nc.dram_tensor(name, list(shape), dt, kind="ExternalInput")

    x_in = din("x", (T, D))
    a_w = {m: din("a_w" + m, (D, D)) for m in "qkvgo"}
    a_b = {m: din("a_b" + m, (1, D)) for m in "qkvgo"}
    gnw_d = din("gnw", (P, D))
    gnb_d = din("gnb", (P, D))
    f_ln1_d = din("f_ln1", (P, DC))
    f_ln2_d = din("f_ln2", (P, DC))
    f_w = {m: din("f_w" + m, (D, D)) for m in "qkvo"}
    f_gate = din("f_gate", (NE, D))
    f_eg = din("f_eg", (NE, EI, D))
    f_eu = din("f_eu", (NE, EI, D))
    f_ed = din("f_ed", (NE, D, EI))
    f_sg = din("f_sg", (SI, D))
    f_su = din("f_su", (SI, D))
    f_sd = din("f_sd", (D, SI))
    ident_d = din("ident", (P, P), BF16)
    identF_d = din("identF", (P, P))
    epsc_d = din("epsc", (P, 2))
    maskS_d = din("maskS", (P, P))
    maskR_d = din("maskR", (P, P))
    ones_d = din("ones_bf", (P, P), BF16)
    gq_d = din("gq", (P, NHR * TT))
    gk_d = din("gk", (P, NHR * TT))
    sin_d = din("sin_sg", (P, TT * HDR))
    cos_d = din("cos_t", (P, TT * HDR))

    y_out = nc.dram_tensor("y", [T, D], F32, kind="ExternalOutput")
    dbg = nc.dram_tensor("dbg", [T, D], F32, kind="ExternalOutput")

    sp_t = nc.dram_tensor("sp_t", [T, D], F32, kind="Internal")
    sp_xA = nc.dram_tensor("sp_xA", [T, D], F32, kind="Internal")
    sp_xB = nc.dram_tensor("sp_xB", [T, D], F32, kind="Internal")
    sp_x2 = nc.dram_tensor("sp_x2", [T, D], F32, kind="Internal")
    sp_ao = nc.dram_tensor("sp_ao", [NHA, P, T], BF16, kind="Internal")
    sp_gate = nc.dram_tensor("sp_gate", [NHR, P, TT * HDR], BF16, kind="Internal")
    wT_eg = nc.dram_tensor("wT_eg", [NE, IC, P, D], BF16, kind="Internal")
    wT_eu = nc.dram_tensor("wT_eu", [NE, IC, P, D], BF16, kind="Internal")
    wT_ed = nc.dram_tensor("wT_ed", [NE, IC, P, D], BF16, kind="Internal")
    wT_sg = nc.dram_tensor("wT_sg", [JC, P, D], BF16, kind="Internal")
    wT_su = nc.dram_tensor("wT_su", [JC, P, D], BF16, kind="Internal")
    wT_sd = nc.dram_tensor("wT_sd", [JC, P, D], BF16, kind="Internal")

    ctx = ExitStack()
    kb = KB(nc, ctx)
    E, S, W = kb.E, kb.sem, kb.W

    B1 = ctx.enter_context(nc.sbuf_tensor("B1", [P, 61440], BF16))
    B2 = ctx.enter_context(nc.sbuf_tensor("B2", [P, 16384], F32))
    B5 = ctx.enter_context(nc.sbuf_tensor("B5", [P, 2048], BF16))
    B6 = ctx.enter_context(nc.sbuf_tensor("B6", [P, 2048], F32))
    ident = ctx.enter_context(nc.sbuf_tensor("identS", [P, P], BF16))
    identF = ctx.enter_context(nc.sbuf_tensor("identFS", [P, P], F32))
    epsc = ctx.enter_context(nc.sbuf_tensor("epscS", [P, 2], F32))
    ones = ctx.enter_context(nc.sbuf_tensor("ones", [P, P], BF16))
    maskS = ctx.enter_context(nc.sbuf_tensor("maskSS", [P, P], F32))
    maskR = ctx.enter_context(nc.sbuf_tensor("maskRS", [P, P], F32))
    ln1s = ctx.enter_context(nc.sbuf_tensor("ln1s", [P, DC], F32))
    ln2s = ctx.enter_context(nc.sbuf_tensor("ln2s", [P, DC], F32))
    gqs = ctx.enter_context(nc.sbuf_tensor("gqs", [P, NHR * TT], F32))
    gks = ctx.enter_context(nc.sbuf_tensor("gks", [P, NHR * TT], F32))
    gwT = ctx.enter_context(nc.sbuf_tensor("gwT", [P, DC * NE], BF16))
    ST = ctx.enter_context(nc.sbuf_tensor("STx", [P, 64], F32))
    CW = ctx.enter_context(nc.sbuf_tensor("CW", [P, TTH * NE], F32))

    M = Rot([PSlot(kb, f"M{i}", [P, 1024], F32) for i in range(3)])
    TSR = Rot([PSlot(kb, f"Tp{i}", [P, 1024], BF16) for i in range(2)])

    qs, qc, qg = S("qs"), S("qc"), S("qg")

    def dma(qeng, qsem, out, in_, waits=(), **kw):
        return kb.E(qeng, "dma_start", out, in_, waits=waits, inc=qsem,
                    incn=16, **kw)

    s_st = S("s_st")
    s_ev = S("s_ev")
    s_ac = S("s_ac")
    s_pe = S("s_pe")
    s_ht = S("s_ht")
    s_po = S("s_po")
    kb.chain_def = {"vector": s_ev, "scalar": s_ac, "gpsimd": s_po}

    def PE():
        return kb.last_pe

    def DV():
        return kb.chain_prev.get("vector")

    def PL():
        return kb.chain_prev.get("gpsimd")

    def b1(a, n):
        return B1[:, a:a + n]

    def b2(a, n):
        return B2[:, a:a + n]


    def eps_ap(eps):
        return epsc[:, 0:1] if eps >= 1e-5 else epsc[:, 1:2]

    def stat_chain(scol, ssqcol, mcol, rcol, eps, n):
        E("vector", "tensor_scalar", mcol, scol, 1.0 / n, None, op0=ALU.mult,
          waits=[(s_ac, s_ac.v)], inc=s_ev)
        E("vector", "tensor_scalar", ST[:, 60:61], ssqcol, 1.0 / n, None,
          op0=ALU.mult, inc=s_ev)
        E("vector", "tensor_tensor", ST[:, 61:62], mcol, mcol, op=ALU.mult,
          inc=s_ev)
        E("vector", "tensor_tensor", ST[:, 62:63], ST[:, 60:61], ST[:, 61:62],
          op=ALU.subtract, inc=s_ev)
        E("scalar", "activation", ST[:, 63:64], ST[:, 62:63], AF.Ln,
          bias=eps_ap(eps), waits=[(s_ev, s_ev.v)], inc=s_ac)
        E("scalar", "activation", rcol, ST[:, 63:64], AF.Exp, scale=-0.5,
          inc=s_ac)

    def rms_chain(ssqcol, rcol, eps):
        E("vector", "tensor_scalar", ST[:, 60:61], ssqcol, 1.0 / D, None,
          op0=ALU.mult, waits=[(s_ac, s_ac.v)], inc=s_ev)
        E("scalar", "activation", ST[:, 63:64], ST[:, 60:61], AF.Ln,
          bias=eps_ap(eps), waits=[(s_ev, s_ev.v)], inc=s_ac)
        E("scalar", "activation", rcol, ST[:, 63:64], AF.Exp, scale=-0.5,
          inc=s_ac)

    def norm_transpose(loader, dst_base, dst_cols, nt, ln_pc, do_ln, do_rms,
                       spill=None, t_first=0):
        tiles = {}
        spill_tgts = {}
        for i in range(min(2, nt)):
            tiles[i] = loader(i)
        for i in range(nt):
            src, swaits = tiles[i]
            if do_ln:
                E("vector", "reduce_sum", ST[:, 0:1], src, axis=AX.X,
                  waits=list(swaits), inc=None)
                pw = [PE()] if PE() else []
                E("scalar", "activation", B5[:, :], src, AF.Square,
                  accum_out=ST[:, 1:2],
                  waits=list(swaits) + pw, inc=None)
                stat_chain(ST[:, 0:1], ST[:, 1:2], ST[:, 2:3], ST[:, 7:8],
                           EPS_LN, D)
                if do_rms:
                    lo = b2(4096 + (i % 2) * 2048, 2048)
                    w = [(s_ac, s_ac.v)]
                    if i >= 2 and (i - 2) in spill_tgts:
                        w.append(spill_tgts[i - 2])
                    E("vector", "tensor_scalar", lo, src, ST[:, 2:3],
                      ST[:, 7:8], op0=ALU.subtract, op1=ALU.mult, waits=w,
                      inc=s_ev)
                    if spill is not None:
                        tt = t_first + i
                        ssem = S(f"spt{i % 2}")
                        spill_tgts[i] = (ssem, dma(
                            "sync", ssem, spill[tt * P:(tt + 1) * P, :], lo,
                            waits=[(s_ev, s_ev.v)]))
                    nsrc, nwaits = lo, [(s_ev, s_ev.v)]
                else:
                    pw = [PE()] if PE() else []
                    E("vector", "tensor_scalar", B5[:, :], src, ST[:, 2:3],
                      ST[:, 7:8], op0=ALU.subtract, op1=ALU.mult,
                      waits=[(s_ac, s_ac.v)] + pw, inc=s_ev)
                    nsrc = None
            else:
                nsrc, nwaits = src, list(swaits)
            if do_rms:
                pw = [PE()] if PE() else []
                E("scalar", "activation", B5[:, :], nsrc, AF.Square,
                  accum_out=ST[:, 8:9],
                  waits=nwaits + pw, inc=s_ac)
                rms_chain(ST[:, 8:9], ST[:, 11:12], EPS_RMS)
                E("vector", "tensor_scalar", B5[:, :], nsrc, ST[:, 11:12],
                  None, op0=ALU.mult, waits=[(s_ac, s_ac.v)], inc=s_ev)
            if i + 2 < nt:
                tiles[i + 2] = loader(i + 2)
            xn_ready = s_ev.v
            tt = t_first + i
            for g in range(2):
                ts = TSR.next()
                w = ts.begin() + [(s_ev, xn_ready)]
                for j in range(8):
                    c = g * 8 + j
                    E("tensor", "transpose", ts.ps[:, j * P:(j + 1) * P],
                      B5[:, c * P:(c + 1) * P], ident[:, :],
                      waits=(w if j == 0 else ()),
                      inc=(ts.done if j == 7 else None))
                for j in range(8):
                    c = g * 8 + j
                    dcol = dst_base + c * dst_cols + tt * P
                    if ln_pc is not None:
                        E("vector", "tensor_scalar", b1(dcol, P),
                          ts.ps[:, j * P:(j + 1) * P], ln_pc[:, c:c + 1], None,
                          op0=ALU.mult, waits=(ts.rd() if j == 0 else ()),
                          inc=(ts.free if j == 7 else None))
                    else:
                        E("vector", "tensor_copy", b1(dcol, P),
                          ts.ps[:, j * P:(j + 1) * P],
                          waits=(ts.rd() if j == 0 else ()),
                          inc=(ts.free if j == 7 else None))
        E("vector", "engine_nop", inc=s_ht)


    def ph_consts():
        cst, cstg = S("cst"), S("cstg")
        dma("sync", cst, ident[:, :], ident_d[:, :])
        dma("sync", cst, identF[:, :], identF_d[:, :])
        dma("sync", cst, epsc[:, :], epsc_d[:, :])
        dma("sync", cst, ones[:, :], ones_d[:, :])
        dma("sync", cst, maskS[:, :], maskS_d[:, :])
        dma("sync", cst, maskR[:, :], maskR_d[:, :])
        dma("sync", cst, ln1s[:, :], f_ln1_d[:, :])
        dma("sync", cst, ln2s[:, :], f_ln2_d[:, :])
        dma("sync", cst, gqs[:, :], gq_d[:, :])
        tload = dma("sync", cst, gks[:, :], gk_d[:, :])
        t_gw = dma("gpsimd", cstg, B5[0:NE, :], f_gate[:, :])
        for g in range(2):
            ts = TSR.next()
            w = ts.begin() + ([(cstg, t_gw), (cst, tload)] if g == 0 else [])
            for j in range(8):
                c = g * 8 + j
                E("tensor", "transpose", ts.ps[0:P, j * NE:(j + 1) * NE],
                  B5[0:NE, c * P:(c + 1) * P], ident[0:NE, 0:NE],
                  waits=(w if j == 0 else ()), inc=(ts.done if j == 7 else None))
            E("vector", "tensor_copy", gwT[:, g * 8 * NE:(g + 1) * 8 * NE],
              ts.ps[0:P, 0:8 * NE], waits=ts.rd(), inc=ts.free)
        E("vector", "engine_nop", inc=s_ht)
        for eng in ENGS:
            W(eng, [(cst, cst.v), (cstg, cstg.v)])

    def ph_norm1(call):
        src_d = x_in if call == 0 else sp_t

        def loader(i):
            xsem = S(f"xs{i % 2}")
            w = []
            if i >= 2:
                w = [(s_ev, s_ev.v), (s_ac, s_ac.v)]
            elif call == 1:
                w = [(S("t3s"), S("t3s").v)]
            tgt = dma("sync", xsem, b2((i % 2) * 2048, 2048),
                      src_d[i * P:(i + 1) * P, :], waits=w)
            return (b2((i % 2) * 2048, 2048), [(xsem, tgt)])
        norm_transpose(loader, HT, T, TT, ln1s, call == 0, True,
                       spill=(sp_t if call == 0 else None))

    def ph_attn(call):
        s_ao = S("s_ao")
        ht_ready = s_ht.v
        stg_tgt = {}
        wsl_rel = {}
        prel = {}
        for h in range(NHA):
            slot = A_WSL + (h % 2) * 3 * 2048
            wsem = S(f"aws{h % 2}")
            for mi, wmat in enumerate((f_w["q"], f_w["k"], f_w["v"])):
                ap = wmat.rearrange("(c p) n -> p c n", p=P)[:, :, h * HDA:(h + 1) * HDA]
                sb = b1(slot + mi * 2048, 2048).rearrange("p (c n) -> p c n", c=DC)
                w = [wsl_rel[h - 2]] if h >= 2 else [(s_ht, ht_ready)]
                wt3 = dma("gpsimd", wsem, sb, ap, waits=w)
            for mi, (dst, scale) in enumerate(((A_QF, RSQ_HDA), (A_KF, None),
                                               (A_VF, None))):
                for half in range(2):
                    m = M.next()
                    w = m.begin() + [(wsem, wt3), (s_ht, ht_ready)]
                    for c in range(DC):
                        for nn in range(2):
                            E("tensor", "matmul", m.ps[:, nn * 512:(nn + 1) * 512],
                              b1(slot + mi * 2048 + c * P, P),
                              b1(HT + c * T + half * 1024 + nn * 512, 512),
                              start=(c == 0), stop=(c == DC - 1),
                              waits=(w if (c == 0 and nn == 0) else ()),
                              inc=(m.done if (c == DC - 1 and nn == 1) else None))
                    dstap = b1(dst + half * 1024, 1024)
                    if scale is not None:
                        E("vector", "tensor_scalar", dstap, m.ps[:, :], scale,
                          None, op0=ALU.mult, waits=m.rd(), inc=m.free)
                    else:
                        E("vector", "tensor_copy", dstap, m.ps[:, :],
                          waits=m.rd(), inc=m.free)
            wsl_rel[h] = PE()
            E("vector", "engine_nop", inc=s_ev)
            vdone = s_ev.v
            for g in range(2):
                ts = TSR.next()
                w = ts.begin() + [(s_ev, vdone)]
                for j in range(8):
                    c = g * 8 + j
                    E("tensor", "transpose", ts.ps[:, j * P:(j + 1) * P],
                      b1(A_VF + c * P, P), ident[:, :],
                      waits=(w if j == 0 else ()),
                      inc=(ts.done if j == 7 else None))
                E("vector", "tensor_copy", b1(A_VT + g * 1024, 1024),
                  ts.ps[:, :], waits=ts.rd(), inc=ts.free)
            def softmax_qt(qt):
                kext = (qt + 1) * P
                nchunks = (kext + 1023) // 1024
                so = 16 + (qt % 2) * 16
                mref = []
                for kc in range(nchunks):
                    m = M.next()
                    w = m.begin()
                    cw_ = min(1024, kext - kc * 1024)
                    nnn = (cw_ + 511) // 512
                    for nn in range(nnn):
                        nwd = min(512, cw_ - nn * 512)
                        E("tensor", "matmul",
                          m.ps[:, nn * 512:nn * 512 + nwd],
                          b1(A_QF + qt * P, P),
                          b1(A_KF + kc * 1024 + nn * 512, nwd),
                          start=True, stop=True,
                          waits=(w if nn == 0 else ()),
                          inc=(m.done if nn == nnn - 1 else None))
                    mref.append((m, cw_))
                mlast, cwl = mref[-1]
                off = cwl - P
                E("vector", "tensor_tensor", mlast.ps[:, off:off + P],
                  mlast.ps[:, off:off + P], maskS[:, :], op=ALU.add,
                  waits=mlast.rd(), inc=s_ev)
                mask_done = s_ev.v
                for ci, (m, cw_) in enumerate(mref):
                    E("vector", "reduce_max", ST[:, so + ci:so + ci + 1],
                      m.ps[:, 0:cw_], axis=AX.X,
                      waits=m.rd() + [(s_ev, mask_done)], inc=s_ev)
                E("vector", "reduce_max", ST[:, so + 4:so + 5],
                  ST[:, so:so + nchunks], axis=AX.X, negate=True, inc=s_ev)
                negmax = s_ev.v
                pbase = A_PR + (qt % 2) * 2048
                for ci, (m, cw_) in enumerate(mref):
                    ew = [(s_ev, negmax)]
                    if (qt % 2) in prel:
                        ew.append(prel[qt % 2])
                    t_exp = E("scalar", "activation",
                              b1(pbase + ci * 1024, cw_),
                              m.ps[:, 0:cw_], AF.Exp,
                              bias=ST[:, so + 4:so + 5],
                              accum_out=ST[:, so + 6 + ci:so + 7 + ci],
                              waits=ew, inc=s_ac)
                    m.released((s_ac, t_exp))
                E("vector", "reduce_sum", ST[:, so + 10:so + 11],
                  ST[:, so + 6:so + 6 + nchunks], axis=AX.X,
                  waits=[(s_ac, s_ac.v)], inc=s_ev)
                E("vector", "reciprocal", ST[:, so + 11:so + 12],
                  ST[:, so + 10:so + 11], inc=s_ev)
                E("gpsimd", "tensor_scalar", b1(pbase, kext), b1(pbase, kext),
                  ST[:, so + 11:so + 12], None, op0=ALU.mult,
                  waits=[(s_ev, s_ev.v), (s_ac, s_ac.v)], inc=s_po)
                return s_po.v

            def ptpv_qt(qt, pnorm):
                nkt = qt + 1
                pbase = A_PR + (qt % 2) * 2048
                for g in range((nkt + 7) // 8):
                    ts = TSR.next()
                    w = ts.begin() + [(s_po, pnorm)]
                    jn = min(8, nkt - g * 8)
                    for j in range(jn):
                        kt = g * 8 + j
                        E("tensor", "transpose", ts.ps[:, j * P:(j + 1) * P],
                          b1(pbase + kt * P, P), ident[:, :],
                          waits=(w if j == 0 else ()),
                          inc=(ts.done if j == jn - 1 else None))
                    E("vector", "tensor_copy", b1(A_PT + g * 8 * P, jn * P),
                      ts.ps[:, 0:jn * P], waits=ts.rd(), inc=ts.free)
                prel[qt % 2] = PE()
                E("vector", "engine_nop", inc=s_ev)
                pt_done = s_ev.v
                m = M.next()
                w = m.begin() + [(s_ev, pt_done)]
                for kt in range(nkt):
                    E("tensor", "matmul", m.ps[:, 0:P],
                      b1(A_VT + kt * P, P), b1(A_PT + kt * P, P),
                      start=(kt == 0), stop=(kt == nkt - 1),
                      waits=(w if kt == 0 else ()),
                      inc=(m.done if kt == nkt - 1 else None))
                stg = 1024 + (qt % 2) * P
                asem = S(f"aos{qt % 2}")
                wex = list(m.rd())
                if stg in stg_tgt:
                    wex.append(stg_tgt[stg])
                E("vector", "tensor_copy", B5[:, stg:stg + P],
                  m.ps[:, 0:P], waits=wex, inc=m.free)
                E("vector", "engine_nop", inc=s_ao)
                stg_tgt[stg] = (asem, dma(
                    "sync", asem, sp_ao[h, :, qt * P:(qt + 1) * P],
                    B5[:, stg:stg + P], waits=[(s_ao, s_ao.v)]))

            pend = None
            for qt in range(TT):
                pn = softmax_qt(qt)
                if pend is not None:
                    ptpv_qt(pend[0], pend[1])
                pend = (qt, pn)
            ptpv_qt(pend[0], pend[1])
    def ph_wo(call):
        woc = S("woc")
        aw0, aw1 = S("aos0"), S("aos1")
        for c in range(DC):
            wo_full = dma("gpsimd", woc, b1(W_WOCH + c * 2048, 2048),
                          f_w["o"][c * P:(c + 1) * P, :],
                          waits=([(aw0, aw0.v), (aw1, aw1.v)] if c == 0 else []),
                          max_dma_last_dim=8192)
        ao_nops = []
        t1_done = []
        aos0, aos1 = S("aos0"), S("aos1")
        spt0, spt1 = S("spt0"), S("spt1")
        t1l = S("t1l")
        x2sems = (S("x2s0"), S("x2s1"))
        x2s_tgt = {}
        for tt in range(TT):
            slot = W_AOTT + (tt % 2) * 2048
            atsem = S(f"aot{tt % 2}")
            w = [ao_nops[tt - 2]] if tt >= 2 else                 [(aos0, aos0.v), (aos1, aos1.v)]
            t_ao = dma("sync", atsem,
                       b1(slot, 2048).rearrange("p (h n) -> p h n", h=NHA),
                       sp_ao.rearrange("h p q -> p h q")[:, :, tt * P:(tt + 1) * P],
                       waits=w)
            w = [(s_ev, t1_done[tt - 1])] if tt >= 1 else                 [(spt0, spt0.v), (spt1, spt1.v)]
            if tt >= 9:
                w.append((x2sems[0], x2s_tgt[0]))
            t_t1 = dma("scalar", t1l, B6[:, :], sp_t[tt * P:(tt + 1) * P, :],
                       waits=w)
            for dh in range(2):
                m = M.next()
                w = m.begin() + [(atsem, t_ao)]
                if tt == 0 and dh == 0:
                    w += [(woc, wo_full)]
                for c in range(DC):
                    for nn in range(2):
                        E("tensor", "matmul", m.ps[:, nn * 512:(nn + 1) * 512],
                          b1(slot + c * P, P),
                          b1(W_WOCH + c * 2048 + dh * 1024 + nn * 512, 512),
                          start=(c == 0), stop=(c == DC - 1),
                          waits=(w if (c == 0 and nn == 0) else ()),
                          inc=(m.done if (c == DC - 1 and nn == 1) else None))
                if dh == 1:
                    ao_nops.append(PE())
                xw = m.rd() + [(t1l, t_t1)]
                if tt == 0 and dh == 0:
                    xw += [(S("xbs"), S("xbs").v), (S("t3s"), S("t3s").v)]
                if tt < TTH:
                    xdst = b2(tt * 2048 + dh * 1024, 1024)
                else:
                    xdst = B6[:, dh * 1024:(dh + 1) * 1024]
                E("vector", "scalar_tensor_tensor", xdst, m.ps[:, :], 1.0,
                  B6[:, dh * 1024:(dh + 1) * 1024], op0=ALU.mult, op1=ALU.add,
                  waits=xw, inc=m.free)
            E("vector", "engine_nop", inc=s_ev)
            t1_done.append(s_ev.v)
            if tt >= TTH:
                x2s_tgt[0] = dma(
                    "sync", x2sems[0], sp_x2[tt * P:(tt + 1) * P, :],
                    B6[:, :], waits=[(s_ev, t1_done[-1])])
        xpp = s_ev.v

        def loader(i):
            return (b2(i * 2048, 2048), [(s_ev, xpp)])
        norm_transpose(loader, W_H2T, TH, TTH, ln2s, False, True)

    def ph_moe_half(call, half):
        produce = (call == 0 and half == 0)
        if half == 1:
            fin_prev = s_ev.v
            x2sems = (S("x2s0"), S("x2s1"))

            def loader(i):
                xsem = S(f"xs{i % 2}")
                if i < 2:
                    w = [(s_ev, fin_prev),
                         (x2sems[0], x2sems[0].v), (x2sems[1], x2sems[1].v)]
                else:
                    w = [(s_ev, s_ev.v), (s_ac, s_ac.v)]
                tgt = dma("sync", xsem, b2(i * 2048, 2048),
                          sp_x2[(TTH + i) * P:(TTH + i + 1) * P, :], waits=w)
                return (b2(i * 2048, 2048), [(xsem, tgt)])
            norm_transpose(loader, M_H2T, TH, TTH, ln2s, False, True)
        h2t_ready = s_ht.v

        m = M.next()
        w = m.begin() + [(s_ht, h2t_ready)]
        for c in range(DC):
            for nn in range(2):
                E("tensor", "matmul", m.ps[0:NE, nn * 512:(nn + 1) * 512],
                  gwT[:, c * NE:(c + 1) * NE],
                  b1(M_H2T + c * TH + nn * 512, 512),
                  start=(c == 0), stop=(c == DC - 1),
                  waits=(w if (c == 0 and nn == 0) else ()),
                  inc=(m.done if (c == DC - 1 and nn == 1) else None))
        E("vector", "tensor_copy", B6[0:NE, 0:TH], m.ps[0:NE, 0:TH],
          waits=m.rd() + [(S("x2s0"), S("x2s0").v), (S("fx0"), S("fx0").v),
                          (S("t1l"), S("t1l").v), (S("fout"), S("fout").v)],
          inc=m.free)
        E("vector", "engine_nop", inc=s_ev)
        lg_done = s_ev.v
        m2 = M.next()
        w = m2.begin() + [(s_ev, lg_done)]
        for j in range(TTH):
            E("tensor", "transpose", m2.ps[:, j * NE:(j + 1) * NE],
              B6[0:NE, j * P:(j + 1) * P], identF[0:NE, 0:NE],
              waits=(w if j == 0 else ()),
              inc=(m2.done if j == TTH - 1 else None))
        E("vector", "tensor_copy", B6[:, 1024:1024 + TTH * NE],
          m2.ps[:, 0:TTH * NE], waits=m2.rd(), inc=m2.free)
        for j in range(TTH):
            lg = B6[:, 1024 + j * NE:1024 + (j + 1) * NE]
            E("vector", "reduce_max", ST[:, 32:33], lg, axis=AX.X, negate=True,
              inc=s_ev)
            E("scalar", "activation", B6[:, 1280:1280 + NE], lg, AF.Exp,
              bias=ST[:, 32:33], waits=[(s_ev, s_ev.v)], inc=s_ac)
            pr = B6[:, 1280:1280 + NE]
            E("vector", "max", ST[:, 34:42], pr, waits=[(s_ac, s_ac.v)],
              inc=s_ev)
            E("vector", "reduce_sum", ST[:, 42:43], ST[:, 34:40], axis=AX.X,
              inc=s_ev)
            E("vector", "reciprocal", ST[:, 43:44], ST[:, 42:43], inc=s_ev)
            E("vector", "tensor_scalar", B6[:, 1312:1312 + NE], pr,
              ST[:, 39:40], None, op0=ALU.is_ge, inc=s_ev)
            E("vector", "scalar_tensor_tensor", CW[:, j * NE:(j + 1) * NE], pr,
              ST[:, 43:44], B6[:, 1312:1312 + NE], op0=ALU.mult, op1=ALU.mult,
              inc=s_ev)
        E("vector", "engine_nop", inc=s_ev)
        cw_done = s_ev.v
        if upto == "cw" and call == 0 and half == 0:
            t_cw = kb.E("gpsimd", "dma_start",
                        dbg[0:P, 0:TTH * NE + 16 + TTH * NE],
                        None, waits=[], inc=None) if False else None
            t_cw = kb.E("gpsimd", "dma_start", dbg[0:P, 0:TTH * NE],
                        CW[:, :], waits=[(s_ev, cw_done)], inc=S("dbgq"),
                        incn=16)
            t_lg = kb.E("gpsimd", "dma_start", dbg[0:P, 256:256 + TTH * NE],
                        B6[:, 1024:1024 + TTH * NE],
                        waits=[(s_ev, cw_done)], inc=S("dbgq"), incn=16)
            W("sync", [(S("dbgq"), t_lg)])

        s_xp = S("s_xp")
        moe_w = (s_ht, h2t_ready)
        nslot = Rot([M_WPC, M_WPC + 2048])
        xslot = Rot([M_XP, M_XP + 2048, M_XP + 4096])
        nat_rel = {}
        x_rel = {}

        def spl_sem(dst_off):
            if dst_off >= M_XP:
                return S(f"pxs{(dst_off - M_XP) // 2048}")
            return S("edsp")

        def make_piece(dst_off, src_nat_ap, wT_dst, extra=()):
            ns = nslot.next()
            nidx = (ns - M_WPC) // 2048
            nsem = S(f"nat{nidx}")
            w = list(extra) + [moe_w]
            if nidx in nat_rel:
                w.append(nat_rel[nidx])
            t_load = dma("gpsimd", nsem, b1(ns, 2048), src_nat_ap, waits=w,
                         max_dma_last_dim=8192)
            ss = spl_sem(dst_off)
            for g in range(2):
                ts = TSR.next()
                w2 = ts.begin() + ([(nsem, t_load)] if g == 0 else [])
                for j in range(8):
                    c = g * 8 + j
                    E("tensor", "transpose", ts.ps[:, j * P:(j + 1) * P],
                      b1(ns + c * P, P), ident[:, :],
                      waits=(w2 if j == 0 else ()),
                      inc=(ts.done if j == 7 else None))
                ew = list(ts.rd()) + (list(extra) if g == 0 else [])
                if g == 0:
                    ew.append((ss, ss.v))
                if g == 1:
                    tgt_xp = E("vector", "tensor_copy",
                               b1(dst_off + g * 1024, 1024),
                               ts.ps[:, :], waits=ew, inc=s_xp)
                    ts.released((s_xp, tgt_xp))
                else:
                    E("vector", "tensor_copy", b1(dst_off + g * 1024, 1024),
                      ts.ps[:, :], waits=ew, inc=ts.free)
            nat_rel[nidx] = PE()
            if wT_dst is not None:
                dma("scalar", ss, wT_dst, b1(dst_off, 2048),
                    waits=[(s_xp, s_xp.v)])
            return [(s_xp, s_xp.v)]

        def load_piece(dst_off, wT_src, extra=(), alt=False):
            if dst_off >= M_XP:
                qq = S(f"pxl{(dst_off - M_XP) // 2048}")
            else:
                qq = S("edl")
            qeng = "scalar" if alt else "sync"
            ss = spl_sem(dst_off)
            tgt = dma(qeng, qq, b1(dst_off, 2048), wT_src,
                      waits=list(extra) + [moe_w, (ss, ss.v)])
            return [(qq, tgt)]

        def gu_mm(pw, rhs_base):
            m = M.next()
            w = m.begin() + pw
            for c in range(DC):
                for nn in range(2):
                    E("tensor", "matmul", m.ps[:, nn * 512:(nn + 1) * 512],
                      b1(rhs_base + c * P, P),
                      b1(M_H2T + c * TH + nn * 512, 512),
                      start=(c == 0), stop=(c == DC - 1),
                      waits=(w if (c == 0 and nn == 0) else ()),
                      inc=(m.done if (c == DC - 1 and nn == 1) else None))
            x_rel[(rhs_base - M_XP) // 2048] = PE()
            return m

        s_hh = S("s_hh")
        down_done = [None]

        tap_hh = [upto == "hh"]

        def expert_block(mk_gu, pieces_d_fn, hh_base, ncontract, cw_col):
            nch = ncontract
            for icx in range(nch):
                pw_g, xg = mk_gu(icx, 0)
                mg = gu_mm(pw_g, xg)
                sw = list(mg.rd()) + [(s_hh, s_hh.v)]
                t_si = E("scalar", "activation", b1(M_GS, 1024), mg.ps[:, :],
                         AF.Silu, waits=sw, inc=s_ac)
                mg.released((s_ac, t_si))
                pw_u, xu = mk_gu(icx, 1)
                mu = gu_mm(pw_u, xu)
                hw = list(mu.rd()) + [(s_ac, s_ac.v)]
                if icx == 0 and down_done[0] is not None:
                    hw.append(down_done[0])
                E("vector", "tensor_tensor", b1(hh_base + icx * 1024, 1024),
                  b1(M_GS, 1024), mu.ps[:, :], op=ALU.mult, waits=hw,
                  inc=mu.free)
                E("vector", "engine_nop", inc=s_hh)
            pieces_d = pieces_d_fn()
            hh_done = s_hh.v
            if upto == "hh" and tap_hh[0]:
                tap_hh[0] = False
                t_hh = kb.E("gpsimd", "dma_start",
                            dbg.rearrange("(a p) d -> p a d", p=P)[:, 0:11, 0:1024],
                            B1[:, M_HH:M_HH + 11 * 1024].rearrange(
                                "p (a d) -> p a d", a=11),
                            waits=[(s_hh, hh_done)], inc=S("dbgq"), incn=16)
                W("sync", [(S("dbgq"), t_hh)])
            dwm = {}
            for sub in pieces_d:
                for (sm, v) in sub:
                    dwm[sm.name] = (sm, max(v, dwm.get(sm.name, (sm, 0))[1]))
            dw = list(dwm.values())
            for tt in range(TTH):
                for dh in range(2):
                    m = M.next()
                    w = m.begin() + dw + [(s_hh, hh_done), (s_ev, cw_done)]
                    w = [x for x in w if x is not None]
                    for icx in range(nch):
                        for nn in range(2):
                            E("tensor", "matmul",
                              m.ps[:, nn * 512:(nn + 1) * 512],
                              b1(hh_base + icx * 1024 + tt * P, P),
                              b1(M_EDT + icx * 2048 + dh * 1024 + nn * 512, 512),
                              start=(icx == 0), stop=(icx == nch - 1),
                              waits=(w if (icx == 0 and nn == 0) else ()),
                              inc=(m.done if (icx == nch - 1 and nn == 1) else None))
                    if cw_col is not None:
                        E("vector", "scalar_tensor_tensor",
                          b2(tt * 2048 + dh * 1024, 1024), m.ps[:, :],
                          cw_col(tt), b2(tt * 2048 + dh * 1024, 1024),
                          op0=ALU.mult, op1=ALU.add,
                          waits=m.rd(), inc=m.free)
                    else:
                        E("vector", "scalar_tensor_tensor",
                          b2(tt * 2048 + dh * 1024, 1024), m.ps[:, :], 1.0,
                          b2(tt * 2048 + dh * 1024, 1024),
                          op0=ALU.mult, op1=ALU.add, waits=m.rd(), inc=m.free)
            down_done[0] = PE()

        def xex(xg):
            idx = (xg - M_XP) // 2048
            return [x_rel[idx]] if idx in x_rel else []

        for e in range(NE):
            def mk_gu(ic, which, _e=e):
                xg = xslot.next()
                if which == 0:
                    if produce:
                        pw = make_piece(xg, f_eg[_e, ic * P:(ic + 1) * P, :],
                                        wT_eg[_e, ic], extra=xex(xg))
                    else:
                        pw = load_piece(xg, wT_eg[_e, ic], extra=xex(xg))
                else:
                    if produce:
                        pw = make_piece(xg, f_eu[_e, ic * P:(ic + 1) * P, :],
                                        wT_eu[_e, ic], extra=xex(xg))
                    else:
                        pw = load_piece(xg, wT_eu[_e, ic], extra=xex(xg),
                                        alt=True)
                return pw, xg

            def mk_d(_e=e):
                pd = []
                for ic in range(IC):
                    ex = [down_done[0]] if down_done[0] is not None else []
                    if produce:
                        ap = f_ed[_e].rearrange("(c p) i -> p c i", p=P)[
                            :, :, ic * P:(ic + 1) * P]
                        pd.append(make_piece(M_EDT + ic * 2048, ap,
                                             wT_ed[_e, ic], extra=ex))
                    else:
                        pd.append(load_piece(M_EDT + ic * 2048, wT_ed[_e, ic],
                                             extra=ex, alt=(ic % 2 == 1)))
                return pd

            def cwc(tt, _e=e):
                return CW[:, tt * NE + _e:tt * NE + _e + 1]
            expert_block(mk_gu, mk_d, M_HH, IC, cwc)

        for grp in range(2):
            def mk_gu(jj, which, _g=grp):
                jc = _g * IC + jj
                xg = xslot.next()
                if which == 0:
                    if produce:
                        pw = make_piece(xg, f_sg[jc * P:(jc + 1) * P, :],
                                        wT_sg[jc], extra=xex(xg))
                    else:
                        pw = load_piece(xg, wT_sg[jc], extra=xex(xg))
                else:
                    if produce:
                        pw = make_piece(xg, f_su[jc * P:(jc + 1) * P, :],
                                        wT_su[jc], extra=xex(xg))
                    else:
                        pw = load_piece(xg, wT_su[jc], extra=xex(xg), alt=True)
                return pw, xg

            def mk_d(_g=grp):
                pd = []
                for jj in range(IC):
                    jc = _g * IC + jj
                    ex = [down_done[0]] if down_done[0] is not None else []
                    if produce:
                        ap = f_sd.rearrange("(c p) j -> p c j", p=P)[
                            :, :, jc * P:(jc + 1) * P]
                        pd.append(make_piece(M_EDT + jj * 2048, ap,
                                             wT_sd[jc], extra=ex))
                    else:
                        pd.append(load_piece(M_EDT + jj * 2048, wT_sd[jc],
                                             extra=ex, alt=(jj % 2 == 1)))
                return pd
            expert_block(mk_gu, mk_d, M_HH, IC, None)

        fx0, fout = S("fx0"), S("fout")
        prev_out = [None]
        for i in range(TTH):
            tt = half * TTH + i
            src_ap = (x_in if call == 0 else sp_xB)[tt * P:(tt + 1) * P, :]
            w = [(fout, prev_out[0])] if prev_out[0] is not None else \
                [(s_ev, s_ev.v)]
            if call == 1 and i == 0 and half == 0:
                w += [(S("xbs"), S("xbs").v)]
            t_x0 = dma("scalar", fx0, B6[:, :], src_ap, waits=w)
            E("vector", "scalar_tensor_tensor", B6[:, :], b2(i * 2048, 2048),
              0.5, B6[:, :], op0=ALU.mult, op1=ALU.add,
              waits=[(fx0, t_x0), down_done[0]], inc=s_ev)
            dst = (sp_xA if call == 0 else y_out)[tt * P:(tt + 1) * P, :]
            prev_out[0] = dma("sync", fout, dst, B6[:, :],
                              waits=[(s_ev, s_ev.v)])

    def ph_retnet():
        fout = S("fout")

        def loader(i):
            xsem = S(f"xs{i % 2}")
            w = []
            if i >= 2:
                w = [(s_ev, s_ev.v), (s_ac, s_ac.v)]
            else:
                w = [(fout, fout.v)]
            tgt = dma("sync", xsem, b2((i % 2) * 2048, 2048),
                      sp_xA[i * P:(i + 1) * P, :], waits=w)
            return (b2((i % 2) * 2048, 2048), [(xsem, tgt)])
        norm_transpose(loader, R_HT2, T, TT, None, True, False)
        ht2 = s_ht.v
        rtb = S("rtb")
        dma("sync", rtb, b2(0, 2048), gnw_d[:, :], waits=[(s_ht, ht2)])
        dma("sync", rtb, b2(2048, 2048), gnb_d[:, :])
        dma("sync", rtb, b2(4096, 4096), sin_d[:, :])
        t_cos = dma("sync", rtb, b2(8192, 4096), cos_d[:, :])
        tbl = [(rtb, t_cos)]
        s_gp = S("s_gp")
        gp_stg = {}
        rw_rel = [None]

        rw = S("rw")

        def load_mats(mats, h):
            tg = None
            for mi, mm_ in enumerate(mats):
                ap = a_w[mm_].rearrange("(c p) n -> p c n", p=P)[
                    :, :, h * HDR:(h + 1) * HDR]
                sb = b1(R_RW + mi * 4096, 4096).rearrange(
                    "p (c n) -> p c n", c=DC)
                if rw_rel[0] is not None:
                    w = [rw_rel[0]]
                else:
                    w = [(s_ht, ht2), (S("edsp"), S("edsp").v)]
                tg = dma("gpsimd", rw, sb, ap, waits=w)
            return tg

        nhr_lim = RET_HEADS
        bias_mi = {"q": 0, "k": 1, "v": 2, "g": 3}
        rbia = S("rbia")

        def load_bias(h):
            tg = None
            for mm_, mi in bias_mi.items():
                if rw_rel[0] is not None:
                    w = [rw_rel[0]]
                else:
                    w = [(s_ht, ht2)] + [(S(f"pxs{i}"), S(f"pxs{i}").v)
                                         for i in range(3)]
                tg = dma("gpsimd", rbia,
                         B1[0:1, R_BIAS + mi * HDR:R_BIAS + (mi + 1) * HDR],
                         a_b[mm_][:, h * HDR:(h + 1) * HDR], waits=w)
            return tg

        t_qk = load_mats("qk", 0)
        t_bias = load_bias(0)
        TMP1, TMP2, RBUF = 12288, 14336, 12288
        for h in range(nhr_lim):
            for mi, mm_ in enumerate("qk"):
                dstb = R_QF if mm_ == "q" else R_KF
                gtab = gqs if mm_ == "q" else gks
                bo = R_BIAS + bias_mi[mm_] * HDR
                sinb, cosb = 4096, 8192
                for half in range(2):
                    for tl in range(TTH):
                        tt = half * TTH + tl
                        m = M.next()
                        w = m.begin() + [(rw, t_qk), (s_ht, ht2), (rbia, t_bias)]
                        for c in range(DC):
                            E("tensor", "matmul", m.ps[:, 0:HDR],
                              b1(R_HT2 + c * T + tt * P, P),
                              b1(R_RW + mi * 4096 + c * HDR, HDR),
                              start=(c == 0), stop=False,
                              waits=(w if c == 0 else ()))
                        E("tensor", "matmul", m.ps[:, 0:HDR],
                          ones[0:1, 0:P], B1[0:1, bo:bo + HDR],
                          start=False, stop=True, inc=m.done)
                        ewc = list(m.rd())
                        if tl == 0 and PL() is not None:
                            ewc.append(PL())
                        E("vector", "tensor_copy", b2(TMP1 + tl * HDR, HDR),
                          m.ps[:, 0:HDR], waits=ewc, inc=m.free)
                    t1v = b2(TMP1, 2048)
                    t2v = b2(TMP2, 2048)
                    wrt = list(tbl)
                    if DV() is not None:
                        wrt.append(DV())
                    E("gpsimd", "tensor_copy", evens(t2v), odds(t1v),
                      waits=wrt)
                    E("gpsimd", "tensor_copy", odds(t2v), evens(t1v))
                    E("gpsimd", "tensor_tensor", t2v, t2v,
                      b2(sinb + half * 2048, 2048), op=ALU.mult)
                    E("gpsimd", "tensor_tensor", t1v, t1v,
                      b2(cosb + half * 2048, 2048), op=ALU.mult)
                    E("gpsimd", "tensor_tensor", t1v, t1v, t2v, op=ALU.add)
                    gb = bass.AP(gtab, h * TT + half * TTH,
                                 [[NHR * TT, P], [1, TTH], [0, HDR]])
                    w5 = [(S("gts0"), S("gts0").v), (S("gts1"), S("gts1").v)]
                    if PE() is not None:
                        w5.append(PE())
                    E("gpsimd", "tensor_tensor",
                      B5[:, :].rearrange("p (s n) -> p s n", s=TTH),
                      t1v.rearrange("p (s n) -> p s n", s=TTH), gb,
                      op=ALU.mult, waits=w5, inc=s_po)
                    rotd = s_po.v
                    for g in range(2):
                        ts = TSR.next()
                        w2 = ts.begin() + ([(s_po, rotd)] if g == 0 else [])
                        for j in range(8):
                            E("tensor", "transpose",
                              ts.ps[:, j * P:(j + 1) * P],
                              B5[:, (g * 8 + j) * P:(g * 8 + j + 1) * P],
                              ident[:, :], waits=(w2 if j == 0 else ()),
                              inc=(ts.done if j == 7 else None))
                        for j in range(8):
                            tl = (g * 8 + j) // 2
                            cc = (g * 8 + j) % 2
                            dcol = dstb + cc * T + (half * TTH + tl) * P
                            E("vector", "tensor_copy", b1(dcol, P),
                              ts.ps[:, j * P:(j + 1) * P],
                              waits=(ts.rd() if j == 0 else ()),
                              inc=(ts.free if j == 7 else None))
            rw_rel[0] = PE()
            if RET_LVL < 2:
                continue
            t_vg = load_mats("vg", h)
            for mi, mm_ in enumerate("vg"):
                dstb = R_VT if mm_ == "v" else R_GR
                bo = R_BIAS + bias_mi[mm_] * HDR
                for tt in range(TT):
                    m = M.next()
                    w = m.begin() + [(rw, t_vg)]
                    if tt == 0 and mi == 0:
                        w += [(s_ac, s_ac.v)]
                    for c in range(DC):
                        E("tensor", "matmul", m.ps[:, 0:HDR],
                          b1(R_HT2 + c * T + tt * P, P),
                          b1(R_RW + mi * 4096 + c * HDR, HDR),
                          start=(c == 0), stop=False,
                          waits=(w if c == 0 else ()))
                    E("tensor", "matmul", m.ps[:, 0:HDR],
                      ones[0:1, 0:P], B1[0:1, bo:bo + HDR],
                      start=False, stop=True, inc=m.done)
                    E("vector", "tensor_copy", b1(dstb + tt * HDR, HDR),
                      m.ps[:, 0:HDR], waits=m.rd(), inc=m.free)
            rw_rel[0] = PE()
            E("vector", "engine_nop", inc=s_ev)
            vg_done = s_ev.v
            if h < nhr_lim - 1:
                t_qk = load_mats("qk", h + 1)
                t_bias = load_bias(h + 1)
            if RET_LVL < 3:
                continue
            for qt in range(TT):
                nkt = qt + 1
                for kt in range(nkt):
                    m = M.next()
                    w = m.begin() + ([(s_ev, vg_done)] if kt == 0 else [])
                    for cc in range(2):
                        E("tensor", "matmul", m.ps[:, 0:P],
                          b1(R_KF + cc * T + kt * P, P),
                          b1(R_QF + cc * T + qt * P, P),
                          start=(cc == 0), stop=(cc == 1),
                          waits=(w if cc == 0 else ()),
                          inc=(m.done if cc == 1 else None))
                    wse = list(m.rd())
                    if h == 0 and qt == 0:
                        wse += [(S("pxs1"), S("pxs1").v),
                                (S("pxs2"), S("pxs2").v)]
                    if kt == qt:
                        E("vector", "scalar_tensor_tensor",
                          b1(R_SIMT + kt * P, P), m.ps[:, 0:P], 1.0,
                          maskR[:, :], op0=ALU.mult, op1=ALU.mult,
                          waits=wse, inc=m.free)
                    else:
                        E("vector", "tensor_copy", b1(R_SIMT + kt * P, P),
                          m.ps[:, 0:P], waits=wse, inc=m.free)
                E("vector", "engine_nop", inc=s_ev)
                simd = s_ev.v
                if RET_LVL < 4:
                    continue
                m = M.next()
                w = m.begin() + [(s_ev, simd)]
                for kt in range(nkt):
                    E("tensor", "matmul", m.ps[:, 0:HDR],
                      b1(R_SIMT + kt * P, P), b1(R_VT + kt * HDR, HDR),
                      start=(kt == 0), stop=(kt == nkt - 1),
                      waits=(w if kt == 0 else ()),
                      inc=(m.done if kt == nkt - 1 else None))
                rew = list(m.rd())
                if qt == 0 and PL() is not None:
                    rew.append(PL())
                E("vector", "tensor_copy", b2(RBUF + qt * HDR, HDR),
                  m.ps[:, 0:HDR], waits=rew, inc=m.free)
                E("vector", "reduce_sum", ST[:, qt:qt + 1],
                  b2(RBUF + qt * HDR, HDR), axis=AX.X)
                E("vector", "scalar_tensor_tensor", B6[:, 0:HDR],
                  b2(RBUF + qt * HDR, HDR), 1.0, b2(RBUF + qt * HDR, HDR),
                  op0=ALU.mult, op1=ALU.mult,
                  accum_out=ST[:, 16 + qt:17 + qt])
            E("vector", "tensor_scalar", ST[:, 32:48], ST[:, 0:16],
              1.0 / HDR, None, op0=ALU.mult)
            E("vector", "tensor_scalar", ST[:, 16:32], ST[:, 16:32],
              1.0 / HDR, None, op0=ALU.mult)
            E("vector", "tensor_tensor", ST[:, 0:16], ST[:, 32:48],
              ST[:, 32:48], op=ALU.mult)
            E("vector", "tensor_tensor", ST[:, 16:32], ST[:, 16:32],
              ST[:, 0:16], op=ALU.subtract, inc=s_ev)
            E("scalar", "activation", ST[:, 0:16], ST[:, 16:32], AF.Ln,
              bias=epsc[:, 0:1], waits=[(s_ev, s_ev.v)])
            E("scalar", "activation", ST[:, 48:64], ST[:, 0:16], AF.Exp,
              scale=-0.5, inc=s_ac)
            rb3 = b2(RBUF, 4096).rearrange("p (s n) -> p s n", s=TT)
            mb = bass.AP(ST, 32, [[64, P], [1, TT], [0, HDR]])
            rsb = bass.AP(ST, 48, [[64, P], [1, TT], [0, HDR]])
            gnwb = bass.AP(B2, h * HDR, [[16384, P], [0, TT], [1, HDR]])
            gnbb = bass.AP(B2, 2048 + h * HDR, [[16384, P], [0, TT], [1, HDR]])
            E("gpsimd", "tensor_tensor", rb3, rb3, mb, op=ALU.subtract,
              waits=[(s_ac, s_ac.v), (s_ev, s_ev.v)])
            E("gpsimd", "tensor_tensor", rb3, rb3, rsb, op=ALU.mult)
            E("gpsimd", "tensor_tensor", rb3, rb3, gnwb, op=ALU.mult)
            E("gpsimd", "tensor_tensor", rb3, rb3, gnbb, op=ALU.add,
              inc=s_po)
            yd = s_po.v
            for half in range(2):
                gsem = S(f"gts{half}")
                gprev = S(f"gts{1 - half}")
                E("scalar", "activation", B5[:, :],
                  b1(R_GR + half * 2048, 2048), AF.Silu,
                  waits=[(s_po, yd), PE(), (gprev, gprev.v)], inc=s_ac)
                E("gpsimd", "tensor_tensor", B5[:, :],
                  b2(RBUF + half * 2048, 2048), B5[:, :], op=ALU.mult,
                  waits=[(s_ac, s_ac.v)], inc=s_gp)
                dma("sync", gsem, sp_gate[h, :, half * 2048:(half + 1) * 2048],
                    B5[:, :], waits=[(s_gp, s_gp.v)])
        if not RET_WO2:
            return
        bo2 = S("bo2")
        t_bo = dma("gpsimd", bo2, B1[0:1, G_BO:G_BO + D], a_b["o"][:, :],
                   waits=[PE(), (s_ac, s_ac.v), (s_gp, s_gp.v)])
        wo2c = S("wo2c")
        for c in range(DC):
            wo2_full = dma("gpsimd", wo2c, b1(G_WOCH + c * 2048, 2048),
                           a_w["o"][c * P:(c + 1) * P, :],
                           waits=([PE()] if c == 0 else []),
                           max_dma_last_dim=8192)
        s_gt = S("s_gt")
        s_x1 = S("s_x1")
        x1_done = {}
        t3_tgt = {}
        for thalf in range(2):
            gt_nops = {}
            for h in range(NHR):
                gslot = G_SRC + (h % 2) * 2048
                gls = S(f"gsl{h % 2}")
                w = [gt_nops[h - 2]] if h >= 2 else                     [(S("gts0"), S("gts0").v), (S("gts1"), S("gts1").v)]
                tgl = dma("sync", gls, b1(gslot, 2048),
                          sp_gate[h, :, thalf * 2048:(thalf + 1) * 2048],
                          waits=w)
                for g in range(2):
                    ts = TSR.next()
                    w2 = ts.begin() + ([(gls, tgl)] if g == 0 else [])
                    for j in range(8):
                        tl = g * 8 + j
                        tloc = tl // 2
                        cc = tl % 2
                        E("tensor", "transpose", ts.ps[:, j * P:(j + 1) * P],
                          b1(gslot + tloc * HDR + cc * P, P),
                          ident[:, :], waits=(w2 if j == 0 else ()),
                          inc=(ts.done if j == 7 else None))
                    for j in range(8):
                        tl = g * 8 + j
                        tloc = tl // 2
                        cc = tl % 2
                        dcol = G_GT + (2 * h + cc) * TH + tloc * P
                        E("vector", "tensor_copy", b1(dcol, P),
                          ts.ps[:, j * P:(j + 1) * P],
                          waits=(ts.rd() if j == 0 else ()),
                          inc=(ts.free if j == 7 else None))
                gt_nops[h] = PE()
            E("vector", "engine_nop", inc=s_ev)
            gt_done = s_ev.v
            for tl in range(TTH):
                tt = thalf * TTH + tl
                xls = S(f"x1l{tt % 2}")
                if (tt % 2) in x1_done and x1_done[tt % 2] is not None:
                    w0 = [(s_ev, x1_done[tt % 2])]
                else:
                    w0 = [(s_ev, s_ev.v), (s_ac, s_ac.v)]
                t_x1 = dma("scalar", xls, b2((tt % 2) * 2048, 2048),
                           sp_xA[tt * P:(tt + 1) * P, :], waits=w0)
                for dh in range(2):
                    m = M.next()
                    w = m.begin() + [(s_ev, gt_done), (bo2, t_bo)]
                    if tt == 0:
                        w += [(wo2c, wo2_full)]
                    for c in range(DC):
                        for nn in range(2):
                            E("tensor", "matmul",
                              m.ps[:, nn * 512:(nn + 1) * 512],
                              b1(G_GT + c * TH + tl * P, P),
                              b1(G_WOCH + c * 2048 + dh * 1024 + nn * 512, 512),
                              start=(c == 0), stop=False,
                              waits=(w if (c == 0 and nn == 0) else ()))
                    for nn in range(2):
                        E("tensor", "matmul", m.ps[:, nn * 512:(nn + 1) * 512],
                          ones[0:1, 0:P],
                          B1[0:1, G_BO + dh * 1024 + nn * 512:
                             G_BO + dh * 1024 + nn * 512 + 512],
                          start=False, stop=True,
                          inc=(m.done if nn == 1 else None))
                    ww = list(m.rd()) + [(xls, t_x1)]
                    if dh == 0 and (4096 in t3_tgt):
                        ww.append(t3_tgt[4096])
                        ww.append((s_ac, s_ac.v))
                    E("vector", "scalar_tensor_tensor",
                      b2(4096 + dh * 1024, 1024), m.ps[:, 0:1024], 1.0,
                      b2((tt % 2) * 2048 + dh * 1024, 1024),
                      op0=ALU.mult, op1=ALU.add, waits=ww, inc=m.free)
                E("vector", "engine_nop", inc=s_ev)
                x2done = s_ev.v
                x1_done[tt % 2] = x2done
                xbs_t = dma("sync", S("xbs"), sp_xB[tt * P:(tt + 1) * P, :],
                            b2(4096, 2048), waits=[(s_ev, x2done)])
                xsrc = b2(4096, 2048)
                E("vector", "reduce_sum", ST[:, 0:1], xsrc, axis=AX.X,
                  waits=[(s_ev, x2done)], inc=None)
                E("scalar", "activation", B5[:, :], xsrc, AF.Square,
                  accum_out=ST[:, 1:2], waits=[(s_ev, x2done)], inc=None)
                stat_chain(ST[:, 0:1], ST[:, 1:2], ST[:, 2:3], ST[:, 7:8],
                           EPS_LN, D)
                w4 = [(s_ac, s_ac.v)]
                if 6144 in t3_tgt:
                    w4.append(t3_tgt[6144])
                E("vector", "tensor_scalar", b2(6144, 2048), xsrc, ST[:, 2:3],
                  ST[:, 7:8], op0=ALU.subtract, op1=ALU.mult, waits=w4,
                  inc=s_ev)
                t3_tgt[6144] = (S("t3s"), dma(
                    "sync", S("t3s"), sp_t[tt * P:(tt + 1) * P, :],
                    b2(6144, 2048), waits=[(s_ev, s_ev.v)]))
                t3_tgt[4096] = (S("xbs"), xbs_t)


    def dbg_copy(src_dram, cast=False):
        q = S("dbgq")
        eng = "gpsimd" if cast else "sync"
        wl = [(s_ev, s_ev.v), (s_ac, s_ac.v)]
        for nm, sm in list(kb._sems.items()):
            if nm not in ("dbgq",):
                wl.append((sm, sm.v))
        t = dma(eng, q, dbg[:, :], src_dram, waits=wl)
        W("sync", wl + [(q, t)])

    ph_consts()
    order = ["ln1", "attn1", "wo1", "h2", "cw", "hh", "moe1", "retnet", "full"]
    stop = order.index(upto) if upto in order else len(order) - 1
    if upto == "h2":
        stop = 2
    if upto == "cw":
        stop = 5
    if upto == "hh":
        stop = 5
    ph_norm1(0)
    if upto == "ln1":
        dbg_copy(sp_t[:, :])
    if stop >= 1:
        ph_attn(0)
        if upto == "attn1":
            dbg_copy(sp_ao.rearrange("h p q -> (h p) q"), cast=True)
    if stop >= 2:
        ph_wo(0)
        if upto == "h2":
            t_h2 = kb.E("gpsimd", "dma_start",
                        dbg.rearrange("(a p) d -> p a d", p=P)[:, 0:8, :],
                        B1[:, 0:16384].rearrange("p (a d) -> p a d", a=8),
                        waits=[(s_ht, s_ht.v)], inc=S("dbgq"), incn=16)
            W("sync", [(S("dbgq"), t_h2)])
        if upto == "wo1":
            for i in range(TTH):
                dma("sync", S("dbg2"), sp_x2[i * P:(i + 1) * P, :],
                    b2(i * 2048, 2048), waits=[(s_ht, s_ht.v)])
            dbg_copy(sp_x2[:, :])
    if stop >= 4:
        ph_moe_half(0, 0)
        if upto not in ("cw", "hh"):
            ph_moe_half(0, 1)
        if upto == "moe1":
            dbg_copy(sp_xA[:, :])
    if stop >= 7:
        ph_retnet()
        if upto == "retnet":
            dbg_copy(sp_xB[:, :])
    if stop >= 8:
        ph_norm1(1)
        ph_attn(1)
        ph_wo(1)
        ph_moe_half(1, 0)
        ph_moe_half(1, 1)

    W("sync", [(sm, sm.v) for sm in kb._sems.values()])
    W("vector", [(s_ev, s_ev.v)])
    W("scalar", [(s_ac, s_ac.v)])

    with nc.Block() as block:
        kb.dispatch(block)
    ctx.close()
    return nc, kb.n_instr


def prep_inputs(inputs):
    gi = {k: np.asarray(v) for k, v in inputs.items()}
    c = host_consts()
    base = {
        "a_wq": gi["L0_attn_w_q"], "a_wk": gi["L0_attn_w_k"],
        "a_wv": gi["L0_attn_w_v"], "a_wg": gi["L0_attn_w_g"],
        "a_wo": gi["L0_attn_w_o"],
        "a_bq": gi["L0_attn_b_q"].reshape(1, D),
        "a_bk": gi["L0_attn_b_k"].reshape(1, D),
        "a_bv": gi["L0_attn_b_v"].reshape(1, D),
        "a_bg": gi["L0_attn_b_g"].reshape(1, D),
        "a_bo": gi["L0_attn_b_o"].reshape(1, D),
        "gnw": np.broadcast_to(gi["L0_attn_gn_weight"][None, :], (P, D)),
        "gnb": np.broadcast_to(gi["L0_attn_gn_bias"][None, :], (P, D)),
        "f_ln1": gi["L0_ffn_ln1_w"].reshape(DC, P).T,
        "f_ln2": gi["L0_ffn_ln2_w"].reshape(DC, P).T,
        "f_wq": gi["L0_ffn_wq"], "f_wk": gi["L0_ffn_wk"],
        "f_wv": gi["L0_ffn_wv"], "f_wo": gi["L0_ffn_wo"],
        "f_gate": gi["L0_ffn_gate_w"],
        "f_eg": gi["L0_ffn_eg_w"], "f_eu": gi["L0_ffn_eu_w"],
        "f_ed": gi["L0_ffn_ed_w"],
        "f_sg": gi["L0_ffn_sg_w"], "f_su": gi["L0_ffn_su_w"],
        "f_sd": gi["L0_ffn_sd_w"],
        "ident": c["ident"], "identF": c["identF"], "epsc": c["epsc"], "maskS": c["maskS"], "maskR": c["maskR"],
        "ones_bf": c["ones_bf"], "gq": c["gq"], "gk": c["gk"],
        "sin_sg": c["sin_sg"], "cos_t": c["cos_t"],
    }
    out = {}
    for k, v in base.items():
        if v.dtype == ml_dtypes.bfloat16:
            out[k] = np.ascontiguousarray(v)
        else:
            out[k] = np.ascontiguousarray(v, dtype=np.float32)
    x = np.asarray(gi["L0_attn_x"], np.float32)
    in_maps = []
    for core in range(8):
        m = dict(out)
        m["x"] = np.ascontiguousarray(x[core])
        in_maps.append(m)
    return in_maps


_CACHE = {}


def _get_prog(upto="full"):
    if upto not in _CACHE:
        _CACHE[upto] = build_program(upto)
    return _CACHE[upto]


def kernel(**inputs):
    nc, n = _get_prog("full")
    in_maps = prep_inputs(inputs)
    res = run_bass_kernel_spmd(nc, in_maps, core_ids=list(range(8)))
    return np.stack([np.asarray(r["y"], np.float32) for r in res.results],
                    axis=0)


# version 07 (final)
# speedup vs baseline: 1.1786x; 1.1786x over previous
import numpy as np
import ml_dtypes
from contextlib import ExitStack
import concourse.bass as bass
import concourse.mybir as mybir
from concourse.bass_utils import run_bass_kernel_spmd

F32 = mybir.dt.float32
BF16 = mybir.dt.bfloat16
AF = mybir.ActivationFunctionType
ALU = mybir.AluOpType
AX = mybir.AxisListType

P = 128
D = 2048
T = 2048
TT = T // P
DC = D // P
NHA, HDA = 16, 128
NHR, HDR = 8, 256
NE, EI = 16, 1408
IC = EI // P
SI = 2816
JC = SI // P
TH = T // 2
TTH = TH // P
EPS_LN = 1e-5
EPS_RMS = 1e-6
EPS_GN = 1e-5
RSQ_HDA = float(1.0 / np.sqrt(HDA))

ENGS = ("tensor", "vector", "scalar", "gpsimd", "sync")
RACE_DETECT = False
RET_HEADS = NHR
RET_WO2 = True
RET_LVL = 9

HT = 0
A_QF, A_KF, A_VF, A_VT = 32768, 34816, 36864, 38912
A_PR, A_PT = 40960, 45056
A_WSL = 49152
W_H2T = 0
W_WOCH = 16384
W_AOTT = 49152
M_H2T = 0
M_EDT = 16384
M_WPC = 38912
M_GS = 43008
M_HH = 44032
M_XP = 55296
R_HT2 = 0
R_RW = 32768
R_QF, R_KF, R_VT, R_GR = 40960, 45056, 49152, 53248
R_SIMT = 57344
R_BIAS = 59392
G_BO = 53248
G_GT = 0
G_WOCH = 16384
G_SRC = 49152


def host_consts():
    c = {}
    c["ident"] = np.eye(P).astype(ml_dtypes.bfloat16)
    iq = np.arange(P)
    c["maskS"] = np.where(iq[None, :] <= iq[:, None], 0.0, -1e9).astype(np.float32)
    c["maskR"] = (iq[None, :] >= iq[:, None]).astype(np.float32)
    c["ones_bf"] = np.ones((P, P), ml_dtypes.bfloat16)
    c["identF"] = np.eye(P).astype(np.float32)
    c["epsc"] = np.broadcast_to(np.array([[1e-5, 1e-6]], np.float32), (P, 2)).copy()
    gam = 1.0 - np.exp(np.linspace(np.log(1 / 32), np.log(1 / 512), NHR))
    t = np.arange(T, dtype=np.float64)
    gq = np.zeros((P, NHR * TT), np.float32)
    gk = np.zeros((P, NHR * TT), np.float32)
    for h in range(NHR):
        for tt in range(TT):
            ts = t[tt * P:(tt + 1) * P]
            gq[:, h * TT + tt] = (gam[h] ** ts).astype(np.float32)
            gk[:, h * TT + tt] = ((gam[h] ** (-ts)) / 16.0).astype(np.float32)
    c["gq"], c["gk"] = gq, gk
    x = np.linspace(0.0, 1.0, HDR // 2)
    thetas = np.repeat(1.0 / (10000.0 ** x), 2)
    ang = t[:, None] * thetas[None, :]
    sin = np.sin(ang).astype(np.float32)
    cos = np.cos(ang).astype(np.float32)
    sin_sg = sin.copy()
    sin_sg[:, 0::2] *= -1.0
    c["sin_sg"] = np.ascontiguousarray(
        sin_sg.reshape(TT, P, HDR).transpose(1, 0, 2).reshape(P, TT * HDR))
    c["cos_t"] = np.ascontiguousarray(
        cos.reshape(TT, P, HDR).transpose(1, 0, 2).reshape(P, TT * HDR))
    return c


class Sem:
    __slots__ = ("h", "v", "name")

    def __init__(self, h, name):
        self.h, self.v, self.name = h, 0, name

    def bump(self, n=1):
        self.v += n
        return self.v


class KB:
    def __init__(self, nc, ctx):
        self.nc, self.ctx = nc, ctx
        self.ops = {e: [] for e in ENGS}
        self._sems = {}
        self.n_instr = 0
        self.last_pe = None
        self.chain_prev = {"vector": None, "scalar": None}
        self.chain_def = {}

    def sem(self, name):
        if name not in self._sems:
            h = self.ctx.enter_context(self.nc.semaphore(name))
            self._sems[name] = Sem(h, name)
        return self._sems[name]

    def E(self, eng, meth, *args, waits=(), inc=None, incn=1, **kw):
        chain = eng in self.chain_prev and meth != "dma_start"
        if chain and inc is None:
            inc = self.chain_def[eng]
        tgt = inc.bump(incn) if inc is not None else None
        if eng == "tensor" and inc is not None:
            self.last_pe = (inc, tgt)
        if chain:
            if self.chain_prev[eng] is not None:
                waits = tuple(waits) + (self.chain_prev[eng],)
            self.chain_prev[eng] = (inc, tgt)
        wl = tuple(waits)
        for (s_, v_) in wl:
            assert isinstance(v_, int), (meth, s_.name if hasattr(s_, "name") else s_, v_)
        self.n_instr += 1 + len(wl)

        def f(e, _m=meth, _a=args, _k=kw, _wl=wl, _i=inc, _n=incn):
            for (s, v) in _wl:
                e.wait_ge(s.h, v)
            r = getattr(e, _m)(*_a, **_k)
            if _i is not None:
                r.then_inc(_i.h, _n)
        self.ops[eng].append(f)
        return tgt

    def W(self, eng, waits):
        wl = tuple(waits)
        self.n_instr += len(wl)

        def f(e, _wl=wl):
            for (s, v) in _wl:
                e.wait_ge(s.h, v)
        self.ops[eng].append(f)

    def dispatch(self, block):
        for eng in ENGS:
            lst = self.ops[eng]

            def body(e, _lst=lst):
                for f in _lst:
                    f(e)
            getattr(block, eng)(body)


class PSlot:
    def __init__(self, kb, name, shape, dtype):
        self.ps = kb.ctx.enter_context(kb.nc.psum_tensor(name, shape, dtype))
        self.done = kb.sem(name + "_d")
        self.free = kb.sem(name + "_f")
        self.fills = 0
        self.rel = None

    def begin(self):
        if self.fills > 0:
            w = [self.rel if self.rel is not None
                 else (self.free, self.free.v)]
        else:
            w = []
        self.rel = None
        self.fills += 1
        return w

    def released(self, evt):
        self.rel = evt

    def rd(self):
        return [(self.done, self.done.v)]


class Rot:
    def __init__(self, slots):
        self.slots = slots
        self.i = -1

    def next(self):
        self.i = (self.i + 1) % len(self.slots)
        return self.slots[self.i]


def evens(ap2d):
    return ap2d.rearrange("p (n two) -> p two n", two=2)[:, 0:1, :]


def odds(ap2d):
    return ap2d.rearrange("p (n two) -> p two n", two=2)[:, 1:2, :]


def build_program(upto="full"):
    nc = bass.Bass("TRN2", target_bir_lowering=False,
                   detect_race_conditions=RACE_DETECT)

    def din(name, shape, dt=F32):
        return nc.dram_tensor(name, list(shape), dt, kind="ExternalInput")

    x_in = din("x", (T, D))
    a_w = {m: din("a_w" + m, (D, D)) for m in "qkvgo"}
    a_b = {m: din("a_b" + m, (1, D)) for m in "qkvgo"}
    gnw_d = din("gnw", (P, D))
    gnb_d = din("gnb", (P, D))
    f_ln1_d = din("f_ln1", (P, DC))
    f_ln2_d = din("f_ln2", (P, DC))
    f_w = {m: din("f_w" + m, (D, D)) for m in "qkvo"}
    f_gate = din("f_gate", (NE, D))
    f_eg = din("f_eg", (NE, EI, D))
    f_eu = din("f_eu", (NE, EI, D))
    f_ed = din("f_ed", (NE, D, EI))
    f_sg = din("f_sg", (SI, D))
    f_su = din("f_su", (SI, D))
    f_sd = din("f_sd", (D, SI))
    ident_d = din("ident", (P, P), BF16)
    identF_d = din("identF", (P, P))
    epsc_d = din("epsc", (P, 2))
    maskS_d = din("maskS", (P, P))
    maskR_d = din("maskR", (P, P))
    ones_d = din("ones_bf", (P, P), BF16)
    gq_d = din("gq", (P, NHR * TT))
    gk_d = din("gk", (P, NHR * TT))
    sin_d = din("sin_sg", (P, TT * HDR))
    cos_d = din("cos_t", (P, TT * HDR))

    y_out = nc.dram_tensor("y", [T, D], F32, kind="ExternalOutput")
    dbg = nc.dram_tensor("dbg", [T, D], F32, kind="ExternalOutput")

    sp_t = nc.dram_tensor("sp_t", [T, D], F32, kind="Internal")
    sp_xA = nc.dram_tensor("sp_xA", [T, D], F32, kind="Internal")
    sp_xB = nc.dram_tensor("sp_xB", [T, D], F32, kind="Internal")
    sp_x2 = nc.dram_tensor("sp_x2", [T, D], F32, kind="Internal")
    sp_ao = nc.dram_tensor("sp_ao", [NHA, P, T], BF16, kind="Internal")
    sp_gate = nc.dram_tensor("sp_gate", [NHR, P, TT * HDR], BF16, kind="Internal")
    wT_eg = nc.dram_tensor("wT_eg", [NE, IC, P, D], BF16, kind="Internal")
    wT_eu = nc.dram_tensor("wT_eu", [NE, IC, P, D], BF16, kind="Internal")
    wT_ed = nc.dram_tensor("wT_ed", [NE, IC, P, D], BF16, kind="Internal")
    wT_sg = nc.dram_tensor("wT_sg", [JC, P, D], BF16, kind="Internal")
    wT_su = nc.dram_tensor("wT_su", [JC, P, D], BF16, kind="Internal")
    wT_sd = nc.dram_tensor("wT_sd", [JC, P, D], BF16, kind="Internal")

    ctx = ExitStack()
    kb = KB(nc, ctx)
    E, S, W = kb.E, kb.sem, kb.W

    B1 = ctx.enter_context(nc.sbuf_tensor("B1", [P, 61440], BF16))
    B2 = ctx.enter_context(nc.sbuf_tensor("B2", [P, 16384], F32))
    B5 = ctx.enter_context(nc.sbuf_tensor("B5", [P, 2048], BF16))
    B6 = ctx.enter_context(nc.sbuf_tensor("B6", [P, 2048], F32))
    ident = ctx.enter_context(nc.sbuf_tensor("identS", [P, P], BF16))
    identF = ctx.enter_context(nc.sbuf_tensor("identFS", [P, P], F32))
    epsc = ctx.enter_context(nc.sbuf_tensor("epscS", [P, 2], F32))
    ones = ctx.enter_context(nc.sbuf_tensor("ones", [P, P], BF16))
    maskS = ctx.enter_context(nc.sbuf_tensor("maskSS", [P, P], F32))
    maskR = ctx.enter_context(nc.sbuf_tensor("maskRS", [P, P], F32))
    ln1s = ctx.enter_context(nc.sbuf_tensor("ln1s", [P, DC], F32))
    ln2s = ctx.enter_context(nc.sbuf_tensor("ln2s", [P, DC], F32))
    gqs = ctx.enter_context(nc.sbuf_tensor("gqs", [P, NHR * TT], F32))
    gks = ctx.enter_context(nc.sbuf_tensor("gks", [P, NHR * TT], F32))
    gwT = ctx.enter_context(nc.sbuf_tensor("gwT", [P, DC * NE], BF16))
    ST = ctx.enter_context(nc.sbuf_tensor("STx", [P, 64], F32))
    CW = ctx.enter_context(nc.sbuf_tensor("CW", [P, TTH * NE], F32))

    M = Rot([PSlot(kb, f"M{i}", [P, 1024], F32) for i in range(3)])
    TSR = Rot([PSlot(kb, f"Tp{i}", [P, 1024], BF16) for i in range(2)])

    qs, qc, qg = S("qs"), S("qc"), S("qg")

    def dma(qeng, qsem, out, in_, waits=(), **kw):
        return kb.E(qeng, "dma_start", out, in_, waits=waits, inc=qsem,
                    incn=16, **kw)

    s_st = S("s_st")
    s_ev = S("s_ev")
    s_ac = S("s_ac")
    s_pe = S("s_pe")
    s_ht = S("s_ht")
    kb.chain_def = {"vector": s_ev, "scalar": s_ac}

    def PE():
        return kb.last_pe

    def b1(a, n):
        return B1[:, a:a + n]

    def b2(a, n):
        return B2[:, a:a + n]


    def eps_ap(eps):
        return epsc[:, 0:1] if eps >= 1e-5 else epsc[:, 1:2]

    def stat_chain(scol, ssqcol, mcol, rcol, eps, n):
        E("vector", "tensor_scalar", mcol, scol, 1.0 / n, None, op0=ALU.mult,
          waits=[(s_ac, s_ac.v)], inc=s_ev)
        E("vector", "tensor_scalar", ST[:, 60:61], ssqcol, 1.0 / n, None,
          op0=ALU.mult, inc=s_ev)
        E("vector", "tensor_tensor", ST[:, 61:62], mcol, mcol, op=ALU.mult,
          inc=s_ev)
        E("vector", "tensor_tensor", ST[:, 62:63], ST[:, 60:61], ST[:, 61:62],
          op=ALU.subtract, inc=s_ev)
        E("scalar", "activation", ST[:, 63:64], ST[:, 62:63], AF.Ln,
          bias=eps_ap(eps), waits=[(s_ev, s_ev.v)], inc=s_ac)
        E("scalar", "activation", rcol, ST[:, 63:64], AF.Exp, scale=-0.5,
          inc=s_ac)

    def rms_chain(ssqcol, rcol, eps):
        E("vector", "tensor_scalar", ST[:, 60:61], ssqcol, 1.0 / D, None,
          op0=ALU.mult, waits=[(s_ac, s_ac.v)], inc=s_ev)
        E("scalar", "activation", ST[:, 63:64], ST[:, 60:61], AF.Ln,
          bias=eps_ap(eps), waits=[(s_ev, s_ev.v)], inc=s_ac)
        E("scalar", "activation", rcol, ST[:, 63:64], AF.Exp, scale=-0.5,
          inc=s_ac)

    def norm_transpose(loader, dst_base, dst_cols, nt, ln_pc, do_ln, do_rms,
                       spill=None, t_first=0):
        tiles = {}
        spill_tgts = {}
        for i in range(min(2, nt)):
            tiles[i] = loader(i)
        for i in range(nt):
            src, swaits = tiles[i]
            if do_ln:
                E("vector", "reduce_sum", ST[:, 0:1], src, axis=AX.X,
                  waits=list(swaits), inc=None)
                pw = [PE()] if PE() else []
                E("scalar", "activation", B5[:, :], src, AF.Square,
                  accum_out=ST[:, 1:2],
                  waits=list(swaits) + pw, inc=None)
                stat_chain(ST[:, 0:1], ST[:, 1:2], ST[:, 2:3], ST[:, 7:8],
                           EPS_LN, D)
                if do_rms:
                    lo = b2(4096 + (i % 2) * 2048, 2048)
                    w = [(s_ac, s_ac.v)]
                    if i >= 2 and (i - 2) in spill_tgts:
                        w.append(spill_tgts[i - 2])
                    E("vector", "tensor_scalar", lo, src, ST[:, 2:3],
                      ST[:, 7:8], op0=ALU.subtract, op1=ALU.mult, waits=w,
                      inc=s_ev)
                    if spill is not None:
                        tt = t_first + i
                        ssem = S(f"spt{i % 2}")
                        spill_tgts[i] = (ssem, dma(
                            "sync", ssem, spill[tt * P:(tt + 1) * P, :], lo,
                            waits=[(s_ev, s_ev.v)]))
                    nsrc, nwaits = lo, [(s_ev, s_ev.v)]
                else:
                    pw = [PE()] if PE() else []
                    E("vector", "tensor_scalar", B5[:, :], src, ST[:, 2:3],
                      ST[:, 7:8], op0=ALU.subtract, op1=ALU.mult,
                      waits=[(s_ac, s_ac.v)] + pw, inc=s_ev)
                    nsrc = None
            else:
                nsrc, nwaits = src, list(swaits)
            if do_rms:
                pw = [PE()] if PE() else []
                E("scalar", "activation", B5[:, :], nsrc, AF.Square,
                  accum_out=ST[:, 8:9],
                  waits=nwaits + pw, inc=s_ac)
                rms_chain(ST[:, 8:9], ST[:, 11:12], EPS_RMS)
                E("vector", "tensor_scalar", B5[:, :], nsrc, ST[:, 11:12],
                  None, op0=ALU.mult, waits=[(s_ac, s_ac.v)], inc=s_ev)
            if i + 2 < nt:
                tiles[i + 2] = loader(i + 2)
            xn_ready = s_ev.v
            tt = t_first + i
            for g in range(2):
                ts = TSR.next()
                w = ts.begin() + [(s_ev, xn_ready)]
                for j in range(8):
                    c = g * 8 + j
                    E("tensor", "transpose", ts.ps[:, j * P:(j + 1) * P],
                      B5[:, c * P:(c + 1) * P], ident[:, :],
                      waits=(w if j == 0 else ()),
                      inc=(ts.done if j == 7 else None))
                for j in range(8):
                    c = g * 8 + j
                    dcol = dst_base + c * dst_cols + tt * P
                    if ln_pc is not None:
                        E("vector", "tensor_scalar", b1(dcol, P),
                          ts.ps[:, j * P:(j + 1) * P], ln_pc[:, c:c + 1], None,
                          op0=ALU.mult, waits=(ts.rd() if j == 0 else ()),
                          inc=(ts.free if j == 7 else None))
                    else:
                        E("vector", "tensor_copy", b1(dcol, P),
                          ts.ps[:, j * P:(j + 1) * P],
                          waits=(ts.rd() if j == 0 else ()),
                          inc=(ts.free if j == 7 else None))
        E("vector", "engine_nop", inc=s_ht)


    def ph_consts():
        cst, cstg = S("cst"), S("cstg")
        dma("sync", cst, ident[:, :], ident_d[:, :])
        dma("sync", cst, identF[:, :], identF_d[:, :])
        dma("sync", cst, epsc[:, :], epsc_d[:, :])
        dma("sync", cst, ones[:, :], ones_d[:, :])
        dma("sync", cst, maskS[:, :], maskS_d[:, :])
        dma("sync", cst, maskR[:, :], maskR_d[:, :])
        dma("sync", cst, ln1s[:, :], f_ln1_d[:, :])
        dma("sync", cst, ln2s[:, :], f_ln2_d[:, :])
        dma("sync", cst, gqs[:, :], gq_d[:, :])
        tload = dma("sync", cst, gks[:, :], gk_d[:, :])
        t_gw = dma("gpsimd", cstg, B5[0:NE, :], f_gate[:, :])
        for g in range(2):
            ts = TSR.next()
            w = ts.begin() + ([(cstg, t_gw), (cst, tload)] if g == 0 else [])
            for j in range(8):
                c = g * 8 + j
                E("tensor", "transpose", ts.ps[0:P, j * NE:(j + 1) * NE],
                  B5[0:NE, c * P:(c + 1) * P], ident[0:NE, 0:NE],
                  waits=(w if j == 0 else ()), inc=(ts.done if j == 7 else None))
            E("vector", "tensor_copy", gwT[:, g * 8 * NE:(g + 1) * 8 * NE],
              ts.ps[0:P, 0:8 * NE], waits=ts.rd(), inc=ts.free)
        E("vector", "engine_nop", inc=s_ht)
        for eng in ENGS:
            W(eng, [(cst, cst.v), (cstg, cstg.v)])

    def ph_norm1(call):
        src_d = x_in if call == 0 else sp_t

        def loader(i):
            xsem = S(f"xs{i % 2}")
            w = []
            if i >= 2:
                w = [(s_ev, s_ev.v), (s_ac, s_ac.v)]
            elif call == 1:
                w = [(S("t3s"), S("t3s").v)]
            tgt = dma("sync", xsem, b2((i % 2) * 2048, 2048),
                      src_d[i * P:(i + 1) * P, :], waits=w)
            return (b2((i % 2) * 2048, 2048), [(xsem, tgt)])
        norm_transpose(loader, HT, T, TT, ln1s, call == 0, True,
                       spill=(sp_t if call == 0 else None))

    def ph_attn(call):
        s_ao = S("s_ao")
        ht_ready = s_ht.v
        stg_tgt = {}
        wsl_rel = {}
        prel = {}
        for h in range(NHA):
            slot = A_WSL + (h % 2) * 3 * 2048
            wsem = S(f"aws{h % 2}")
            for mi, wmat in enumerate((f_w["q"], f_w["k"], f_w["v"])):
                ap = wmat.rearrange("(c p) n -> p c n", p=P)[:, :, h * HDA:(h + 1) * HDA]
                sb = b1(slot + mi * 2048, 2048).rearrange("p (c n) -> p c n", c=DC)
                w = [wsl_rel[h - 2]] if h >= 2 else [(s_ht, ht_ready)]
                wt3 = dma("gpsimd", wsem, sb, ap, waits=w)
            for mi, (dst, scale) in enumerate(((A_QF, RSQ_HDA), (A_KF, None),
                                               (A_VF, None))):
                for half in range(2):
                    m = M.next()
                    w = m.begin() + [(wsem, wt3), (s_ht, ht_ready)]
                    for c in range(DC):
                        for nn in range(2):
                            E("tensor", "matmul", m.ps[:, nn * 512:(nn + 1) * 512],
                              b1(slot + mi * 2048 + c * P, P),
                              b1(HT + c * T + half * 1024 + nn * 512, 512),
                              start=(c == 0), stop=(c == DC - 1),
                              waits=(w if (c == 0 and nn == 0) else ()),
                              inc=(m.done if (c == DC - 1 and nn == 1) else None))
                    dstap = b1(dst + half * 1024, 1024)
                    if scale is not None:
                        E("vector", "tensor_scalar", dstap, m.ps[:, :], scale,
                          None, op0=ALU.mult, waits=m.rd(), inc=m.free)
                    else:
                        E("vector", "tensor_copy", dstap, m.ps[:, :],
                          waits=m.rd(), inc=m.free)
            wsl_rel[h] = PE()
            E("vector", "engine_nop", inc=s_ev)
            vdone = s_ev.v
            for g in range(2):
                ts = TSR.next()
                w = ts.begin() + [(s_ev, vdone)]
                for j in range(8):
                    c = g * 8 + j
                    E("tensor", "transpose", ts.ps[:, j * P:(j + 1) * P],
                      b1(A_VF + c * P, P), ident[:, :],
                      waits=(w if j == 0 else ()),
                      inc=(ts.done if j == 7 else None))
                E("vector", "tensor_copy", b1(A_VT + g * 1024, 1024),
                  ts.ps[:, :], waits=ts.rd(), inc=ts.free)
            def softmax_qt(qt):
                kext = (qt + 1) * P
                nchunks = (kext + 1023) // 1024
                so = 16 + (qt % 2) * 16
                mref = []
                for kc in range(nchunks):
                    m = M.next()
                    w = m.begin()
                    cw_ = min(1024, kext - kc * 1024)
                    nnn = (cw_ + 511) // 512
                    for nn in range(nnn):
                        nwd = min(512, cw_ - nn * 512)
                        E("tensor", "matmul",
                          m.ps[:, nn * 512:nn * 512 + nwd],
                          b1(A_QF + qt * P, P),
                          b1(A_KF + kc * 1024 + nn * 512, nwd),
                          start=True, stop=True,
                          waits=(w if nn == 0 else ()),
                          inc=(m.done if nn == nnn - 1 else None))
                    mref.append((m, cw_))
                mlast, cwl = mref[-1]
                off = cwl - P
                E("vector", "tensor_tensor", mlast.ps[:, off:off + P],
                  mlast.ps[:, off:off + P], maskS[:, :], op=ALU.add,
                  waits=mlast.rd(), inc=s_ev)
                mask_done = s_ev.v
                for ci, (m, cw_) in enumerate(mref):
                    E("vector", "reduce_max", ST[:, so + ci:so + ci + 1],
                      m.ps[:, 0:cw_], axis=AX.X,
                      waits=m.rd() + [(s_ev, mask_done)], inc=s_ev)
                E("vector", "reduce_max", ST[:, so + 4:so + 5],
                  ST[:, so:so + nchunks], axis=AX.X, negate=True, inc=s_ev)
                negmax = s_ev.v
                pbase = A_PR + (qt % 2) * 2048
                for ci, (m, cw_) in enumerate(mref):
                    ew = [(s_ev, negmax)]
                    if (qt % 2) in prel:
                        ew.append(prel[qt % 2])
                    t_exp = E("scalar", "activation",
                              b1(pbase + ci * 1024, cw_),
                              m.ps[:, 0:cw_], AF.Exp,
                              bias=ST[:, so + 4:so + 5],
                              accum_out=ST[:, so + 6 + ci:so + 7 + ci],
                              waits=ew, inc=s_ac)
                    m.released((s_ac, t_exp))
                E("vector", "reduce_sum", ST[:, so + 10:so + 11],
                  ST[:, so + 6:so + 6 + nchunks], axis=AX.X,
                  waits=[(s_ac, s_ac.v)], inc=s_ev)
                E("vector", "reciprocal", ST[:, so + 11:so + 12],
                  ST[:, so + 10:so + 11], inc=s_ev)
                E("scalar", "activation", b1(pbase, kext), b1(pbase, kext),
                  AF.Copy, scale=ST[:, so + 11:so + 12],
                  waits=[(s_ev, s_ev.v)], inc=s_ac)
                return s_ac.v

            def ptpv_qt(qt, pnorm):
                nkt = qt + 1
                pbase = A_PR + (qt % 2) * 2048
                for g in range((nkt + 7) // 8):
                    ts = TSR.next()
                    w = ts.begin() + [(s_ac, pnorm)]
                    jn = min(8, nkt - g * 8)
                    for j in range(jn):
                        kt = g * 8 + j
                        E("tensor", "transpose", ts.ps[:, j * P:(j + 1) * P],
                          b1(pbase + kt * P, P), ident[:, :],
                          waits=(w if j == 0 else ()),
                          inc=(ts.done if j == jn - 1 else None))
                    E("vector", "tensor_copy", b1(A_PT + g * 8 * P, jn * P),
                      ts.ps[:, 0:jn * P], waits=ts.rd(), inc=ts.free)
                prel[qt % 2] = PE()
                E("vector", "engine_nop", inc=s_ev)
                pt_done = s_ev.v
                m = M.next()
                w = m.begin() + [(s_ev, pt_done)]
                for kt in range(nkt):
                    E("tensor", "matmul", m.ps[:, 0:P],
                      b1(A_VT + kt * P, P), b1(A_PT + kt * P, P),
                      start=(kt == 0), stop=(kt == nkt - 1),
                      waits=(w if kt == 0 else ()),
                      inc=(m.done if kt == nkt - 1 else None))
                stg = 1024 + (qt % 2) * P
                asem = S(f"aos{qt % 2}")
                wex = list(m.rd())
                if stg in stg_tgt:
                    wex.append(stg_tgt[stg])
                E("vector", "tensor_copy", B5[:, stg:stg + P],
                  m.ps[:, 0:P], waits=wex, inc=m.free)
                E("vector", "engine_nop", inc=s_ao)
                stg_tgt[stg] = (asem, dma(
                    "sync", asem, sp_ao[h, :, qt * P:(qt + 1) * P],
                    B5[:, stg:stg + P], waits=[(s_ao, s_ao.v)]))

            pend = None
            for qt in range(TT):
                pn = softmax_qt(qt)
                if pend is not None:
                    ptpv_qt(pend[0], pend[1])
                pend = (qt, pn)
            ptpv_qt(pend[0], pend[1])
    def ph_wo(call):
        woc = S("woc")
        aw0, aw1 = S("aos0"), S("aos1")
        for c in range(DC):
            wo_full = dma("gpsimd", woc, b1(W_WOCH + c * 2048, 2048),
                          f_w["o"][c * P:(c + 1) * P, :],
                          waits=([(aw0, aw0.v), (aw1, aw1.v)] if c == 0 else []),
                          max_dma_last_dim=8192)
        ao_nops = []
        t1_done = []
        aos0, aos1 = S("aos0"), S("aos1")
        spt0, spt1 = S("spt0"), S("spt1")
        t1l = S("t1l")
        x2sems = (S("x2s0"), S("x2s1"))
        x2s_tgt = {}
        for tt in range(TT):
            slot = W_AOTT + (tt % 2) * 2048
            atsem = S(f"aot{tt % 2}")
            w = [ao_nops[tt - 2]] if tt >= 2 else                 [(aos0, aos0.v), (aos1, aos1.v)]
            t_ao = dma("sync", atsem,
                       b1(slot, 2048).rearrange("p (h n) -> p h n", h=NHA),
                       sp_ao.rearrange("h p q -> p h q")[:, :, tt * P:(tt + 1) * P],
                       waits=w)
            w = [(s_ev, t1_done[tt - 1])] if tt >= 1 else                 [(spt0, spt0.v), (spt1, spt1.v)]
            if tt >= 9:
                w.append((x2sems[0], x2s_tgt[0]))
            t_t1 = dma("scalar", t1l, B6[:, :], sp_t[tt * P:(tt + 1) * P, :],
                       waits=w)
            for dh in range(2):
                m = M.next()
                w = m.begin() + [(atsem, t_ao)]
                if tt == 0 and dh == 0:
                    w += [(woc, wo_full)]
                for c in range(DC):
                    for nn in range(2):
                        E("tensor", "matmul", m.ps[:, nn * 512:(nn + 1) * 512],
                          b1(slot + c * P, P),
                          b1(W_WOCH + c * 2048 + dh * 1024 + nn * 512, 512),
                          start=(c == 0), stop=(c == DC - 1),
                          waits=(w if (c == 0 and nn == 0) else ()),
                          inc=(m.done if (c == DC - 1 and nn == 1) else None))
                if dh == 1:
                    ao_nops.append(PE())
                xw = m.rd() + [(t1l, t_t1)]
                if tt == 0 and dh == 0:
                    xw += [(S("xbs"), S("xbs").v), (S("t3s"), S("t3s").v)]
                if tt < TTH:
                    xdst = b2(tt * 2048 + dh * 1024, 1024)
                else:
                    xdst = B6[:, dh * 1024:(dh + 1) * 1024]
                E("vector", "scalar_tensor_tensor", xdst, m.ps[:, :], 1.0,
                  B6[:, dh * 1024:(dh + 1) * 1024], op0=ALU.mult, op1=ALU.add,
                  waits=xw, inc=m.free)
            E("vector", "engine_nop", inc=s_ev)
            t1_done.append(s_ev.v)
            if tt >= TTH:
                x2s_tgt[0] = dma(
                    "sync", x2sems[0], sp_x2[tt * P:(tt + 1) * P, :],
                    B6[:, :], waits=[(s_ev, t1_done[-1])])
        xpp = s_ev.v

        def loader(i):
            return (b2(i * 2048, 2048), [(s_ev, xpp)])
        norm_transpose(loader, W_H2T, TH, TTH, ln2s, False, True)

    def ph_moe_half(call, half):
        produce = (call == 0 and half == 0)
        if half == 1:
            fin_prev = s_ev.v
            x2sems = (S("x2s0"), S("x2s1"))

            def loader(i):
                xsem = S(f"xs{i % 2}")
                if i < 2:
                    w = [(s_ev, fin_prev),
                         (x2sems[0], x2sems[0].v), (x2sems[1], x2sems[1].v)]
                else:
                    w = [(s_ev, s_ev.v), (s_ac, s_ac.v)]
                tgt = dma("sync", xsem, b2(i * 2048, 2048),
                          sp_x2[(TTH + i) * P:(TTH + i + 1) * P, :], waits=w)
                return (b2(i * 2048, 2048), [(xsem, tgt)])
            norm_transpose(loader, M_H2T, TH, TTH, ln2s, False, True)
        h2t_ready = s_ht.v

        m = M.next()
        w = m.begin() + [(s_ht, h2t_ready)]
        for c in range(DC):
            for nn in range(2):
                E("tensor", "matmul", m.ps[0:NE, nn * 512:(nn + 1) * 512],
                  gwT[:, c * NE:(c + 1) * NE],
                  b1(M_H2T + c * TH + nn * 512, 512),
                  start=(c == 0), stop=(c == DC - 1),
                  waits=(w if (c == 0 and nn == 0) else ()),
                  inc=(m.done if (c == DC - 1 and nn == 1) else None))
        E("vector", "tensor_copy", B6[0:NE, 0:TH], m.ps[0:NE, 0:TH],
          waits=m.rd() + [(S("x2s0"), S("x2s0").v), (S("fx0"), S("fx0").v),
                          (S("t1l"), S("t1l").v), (S("fout"), S("fout").v)],
          inc=m.free)
        E("vector", "engine_nop", inc=s_ev)
        lg_done = s_ev.v
        m2 = M.next()
        w = m2.begin() + [(s_ev, lg_done)]
        for j in range(TTH):
            E("tensor", "transpose", m2.ps[:, j * NE:(j + 1) * NE],
              B6[0:NE, j * P:(j + 1) * P], identF[0:NE, 0:NE],
              waits=(w if j == 0 else ()),
              inc=(m2.done if j == TTH - 1 else None))
        E("vector", "tensor_copy", B6[:, 1024:1024 + TTH * NE],
          m2.ps[:, 0:TTH * NE], waits=m2.rd(), inc=m2.free)
        for j in range(TTH):
            lg = B6[:, 1024 + j * NE:1024 + (j + 1) * NE]
            E("vector", "reduce_max", ST[:, 32:33], lg, axis=AX.X, negate=True,
              inc=s_ev)
            E("scalar", "activation", B6[:, 1280:1280 + NE], lg, AF.Exp,
              bias=ST[:, 32:33], waits=[(s_ev, s_ev.v)], inc=s_ac)
            pr = B6[:, 1280:1280 + NE]
            E("vector", "max", ST[:, 34:42], pr, waits=[(s_ac, s_ac.v)],
              inc=s_ev)
            E("vector", "reduce_sum", ST[:, 42:43], ST[:, 34:40], axis=AX.X,
              inc=s_ev)
            E("vector", "reciprocal", ST[:, 43:44], ST[:, 42:43], inc=s_ev)
            E("vector", "tensor_scalar", B6[:, 1312:1312 + NE], pr,
              ST[:, 39:40], None, op0=ALU.is_ge, inc=s_ev)
            E("vector", "scalar_tensor_tensor", CW[:, j * NE:(j + 1) * NE], pr,
              ST[:, 43:44], B6[:, 1312:1312 + NE], op0=ALU.mult, op1=ALU.mult,
              inc=s_ev)
        E("vector", "engine_nop", inc=s_ev)
        cw_done = s_ev.v
        if upto == "cw" and call == 0 and half == 0:
            t_cw = kb.E("gpsimd", "dma_start",
                        dbg[0:P, 0:TTH * NE + 16 + TTH * NE],
                        None, waits=[], inc=None) if False else None
            t_cw = kb.E("gpsimd", "dma_start", dbg[0:P, 0:TTH * NE],
                        CW[:, :], waits=[(s_ev, cw_done)], inc=S("dbgq"),
                        incn=16)
            t_lg = kb.E("gpsimd", "dma_start", dbg[0:P, 256:256 + TTH * NE],
                        B6[:, 1024:1024 + TTH * NE],
                        waits=[(s_ev, cw_done)], inc=S("dbgq"), incn=16)
            W("sync", [(S("dbgq"), t_lg)])

        s_xp = S("s_xp")
        moe_w = (s_ht, h2t_ready)
        nslot = Rot([M_WPC, M_WPC + 2048])
        xslot = Rot([M_XP, M_XP + 2048, M_XP + 4096])
        nat_rel = {}
        x_rel = {}

        def spl_sem(dst_off):
            if dst_off >= M_XP:
                return S(f"pxs{(dst_off - M_XP) // 2048}")
            return S("edsp")

        def make_piece(dst_off, src_nat_ap, wT_dst, extra=()):
            ns = nslot.next()
            nidx = (ns - M_WPC) // 2048
            nsem = S(f"nat{nidx}")
            w = list(extra) + [moe_w]
            if nidx in nat_rel:
                w.append(nat_rel[nidx])
            t_load = dma("gpsimd", nsem, b1(ns, 2048), src_nat_ap, waits=w,
                         max_dma_last_dim=8192)
            ss = spl_sem(dst_off)
            for g in range(2):
                ts = TSR.next()
                w2 = ts.begin() + ([(nsem, t_load)] if g == 0 else [])
                for j in range(8):
                    c = g * 8 + j
                    E("tensor", "transpose", ts.ps[:, j * P:(j + 1) * P],
                      b1(ns + c * P, P), ident[:, :],
                      waits=(w2 if j == 0 else ()),
                      inc=(ts.done if j == 7 else None))
                ew = list(ts.rd()) + (list(extra) if g == 0 else [])
                if g == 0:
                    ew.append((ss, ss.v))
                if g == 1:
                    tgt_xp = E("vector", "tensor_copy",
                               b1(dst_off + g * 1024, 1024),
                               ts.ps[:, :], waits=ew, inc=s_xp)
                    ts.released((s_xp, tgt_xp))
                else:
                    E("vector", "tensor_copy", b1(dst_off + g * 1024, 1024),
                      ts.ps[:, :], waits=ew, inc=ts.free)
            nat_rel[nidx] = PE()
            if wT_dst is not None:
                dma("scalar", ss, wT_dst, b1(dst_off, 2048),
                    waits=[(s_xp, s_xp.v)])
            return [(s_xp, s_xp.v)]

        def load_piece(dst_off, wT_src, extra=(), alt=False):
            if dst_off >= M_XP:
                qq = S(f"pxl{(dst_off - M_XP) // 2048}")
            else:
                qq = S("edl")
            qeng = "scalar" if alt else "sync"
            ss = spl_sem(dst_off)
            tgt = dma(qeng, qq, b1(dst_off, 2048), wT_src,
                      waits=list(extra) + [moe_w, (ss, ss.v)])
            return [(qq, tgt)]

        def gu_mm(pw, rhs_base):
            m = M.next()
            w = m.begin() + pw
            for c in range(DC):
                for nn in range(2):
                    E("tensor", "matmul", m.ps[:, nn * 512:(nn + 1) * 512],
                      b1(rhs_base + c * P, P),
                      b1(M_H2T + c * TH + nn * 512, 512),
                      start=(c == 0), stop=(c == DC - 1),
                      waits=(w if (c == 0 and nn == 0) else ()),
                      inc=(m.done if (c == DC - 1 and nn == 1) else None))
            x_rel[(rhs_base - M_XP) // 2048] = PE()
            return m

        s_hh = S("s_hh")
        down_done = [None]

        tap_hh = [upto == "hh"]

        def expert_block(mk_gu, pieces_d_fn, hh_base, ncontract, cw_col):
            nch = ncontract
            for icx in range(nch):
                pw_g, xg = mk_gu(icx, 0)
                mg = gu_mm(pw_g, xg)
                sw = list(mg.rd()) + [(s_hh, s_hh.v)]
                t_si = E("scalar", "activation", b1(M_GS, 1024), mg.ps[:, :],
                         AF.Silu, waits=sw, inc=s_ac)
                mg.released((s_ac, t_si))
                pw_u, xu = mk_gu(icx, 1)
                mu = gu_mm(pw_u, xu)
                hw = list(mu.rd()) + [(s_ac, s_ac.v)]
                if icx == 0 and down_done[0] is not None:
                    hw.append(down_done[0])
                E("vector", "tensor_tensor", b1(hh_base + icx * 1024, 1024),
                  b1(M_GS, 1024), mu.ps[:, :], op=ALU.mult, waits=hw,
                  inc=mu.free)
                E("vector", "engine_nop", inc=s_hh)
            pieces_d = pieces_d_fn()
            hh_done = s_hh.v
            if upto == "hh" and tap_hh[0]:
                tap_hh[0] = False
                t_hh = kb.E("gpsimd", "dma_start",
                            dbg.rearrange("(a p) d -> p a d", p=P)[:, 0:11, 0:1024],
                            B1[:, M_HH:M_HH + 11 * 1024].rearrange(
                                "p (a d) -> p a d", a=11),
                            waits=[(s_hh, hh_done)], inc=S("dbgq"), incn=16)
                W("sync", [(S("dbgq"), t_hh)])
            dwm = {}
            for sub in pieces_d:
                for (sm, v) in sub:
                    dwm[sm.name] = (sm, max(v, dwm.get(sm.name, (sm, 0))[1]))
            dw = list(dwm.values())
            for tt in range(TTH):
                for dh in range(2):
                    m = M.next()
                    w = m.begin() + dw + [(s_hh, hh_done), (s_ev, cw_done)]
                    w = [x for x in w if x is not None]
                    for icx in range(nch):
                        for nn in range(2):
                            E("tensor", "matmul",
                              m.ps[:, nn * 512:(nn + 1) * 512],
                              b1(hh_base + icx * 1024 + tt * P, P),
                              b1(M_EDT + icx * 2048 + dh * 1024 + nn * 512, 512),
                              start=(icx == 0), stop=(icx == nch - 1),
                              waits=(w if (icx == 0 and nn == 0) else ()),
                              inc=(m.done if (icx == nch - 1 and nn == 1) else None))
                    if cw_col is not None:
                        E("vector", "scalar_tensor_tensor",
                          b2(tt * 2048 + dh * 1024, 1024), m.ps[:, :],
                          cw_col(tt), b2(tt * 2048 + dh * 1024, 1024),
                          op0=ALU.mult, op1=ALU.add,
                          waits=m.rd(), inc=m.free)
                    else:
                        E("vector", "scalar_tensor_tensor",
                          b2(tt * 2048 + dh * 1024, 1024), m.ps[:, :], 1.0,
                          b2(tt * 2048 + dh * 1024, 1024),
                          op0=ALU.mult, op1=ALU.add, waits=m.rd(), inc=m.free)
            down_done[0] = PE()

        def xex(xg):
            idx = (xg - M_XP) // 2048
            return [x_rel[idx]] if idx in x_rel else []

        for e in range(NE):
            def mk_gu(ic, which, _e=e):
                xg = xslot.next()
                if which == 0:
                    if produce:
                        pw = make_piece(xg, f_eg[_e, ic * P:(ic + 1) * P, :],
                                        wT_eg[_e, ic], extra=xex(xg))
                    else:
                        pw = load_piece(xg, wT_eg[_e, ic], extra=xex(xg))
                else:
                    if produce:
                        pw = make_piece(xg, f_eu[_e, ic * P:(ic + 1) * P, :],
                                        wT_eu[_e, ic], extra=xex(xg))
                    else:
                        pw = load_piece(xg, wT_eu[_e, ic], extra=xex(xg),
                                        alt=True)
                return pw, xg

            def mk_d(_e=e):
                pd = []
                for ic in range(IC):
                    ex = [down_done[0]] if down_done[0] is not None else []
                    if produce:
                        ap = f_ed[_e].rearrange("(c p) i -> p c i", p=P)[
                            :, :, ic * P:(ic + 1) * P]
                        pd.append(make_piece(M_EDT + ic * 2048, ap,
                                             wT_ed[_e, ic], extra=ex))
                    else:
                        pd.append(load_piece(M_EDT + ic * 2048, wT_ed[_e, ic],
                                             extra=ex, alt=(ic % 2 == 1)))
                return pd

            def cwc(tt, _e=e):
                return CW[:, tt * NE + _e:tt * NE + _e + 1]
            expert_block(mk_gu, mk_d, M_HH, IC, cwc)

        for grp in range(2):
            def mk_gu(jj, which, _g=grp):
                jc = _g * IC + jj
                xg = xslot.next()
                if which == 0:
                    if produce:
                        pw = make_piece(xg, f_sg[jc * P:(jc + 1) * P, :],
                                        wT_sg[jc], extra=xex(xg))
                    else:
                        pw = load_piece(xg, wT_sg[jc], extra=xex(xg))
                else:
                    if produce:
                        pw = make_piece(xg, f_su[jc * P:(jc + 1) * P, :],
                                        wT_su[jc], extra=xex(xg))
                    else:
                        pw = load_piece(xg, wT_su[jc], extra=xex(xg), alt=True)
                return pw, xg

            def mk_d(_g=grp):
                pd = []
                for jj in range(IC):
                    jc = _g * IC + jj
                    ex = [down_done[0]] if down_done[0] is not None else []
                    if produce:
                        ap = f_sd.rearrange("(c p) j -> p c j", p=P)[
                            :, :, jc * P:(jc + 1) * P]
                        pd.append(make_piece(M_EDT + jj * 2048, ap,
                                             wT_sd[jc], extra=ex))
                    else:
                        pd.append(load_piece(M_EDT + jj * 2048, wT_sd[jc],
                                             extra=ex, alt=(jj % 2 == 1)))
                return pd
            expert_block(mk_gu, mk_d, M_HH, IC, None)

        fx0, fout = S("fx0"), S("fout")
        prev_out = [None]
        for i in range(TTH):
            tt = half * TTH + i
            src_ap = (x_in if call == 0 else sp_xB)[tt * P:(tt + 1) * P, :]
            w = [(fout, prev_out[0])] if prev_out[0] is not None else \
                [(s_ev, s_ev.v)]
            if call == 1 and i == 0 and half == 0:
                w += [(S("xbs"), S("xbs").v)]
            t_x0 = dma("scalar", fx0, B6[:, :], src_ap, waits=w)
            E("vector", "scalar_tensor_tensor", B6[:, :], b2(i * 2048, 2048),
              0.5, B6[:, :], op0=ALU.mult, op1=ALU.add,
              waits=[(fx0, t_x0), down_done[0]], inc=s_ev)
            dst = (sp_xA if call == 0 else y_out)[tt * P:(tt + 1) * P, :]
            prev_out[0] = dma("sync", fout, dst, B6[:, :],
                              waits=[(s_ev, s_ev.v)])

    def ph_retnet():
        fout = S("fout")

        def loader(i):
            xsem = S(f"xs{i % 2}")
            w = []
            if i >= 2:
                w = [(s_ev, s_ev.v), (s_ac, s_ac.v)]
            else:
                w = [(fout, fout.v)]
            tgt = dma("sync", xsem, b2((i % 2) * 2048, 2048),
                      sp_xA[i * P:(i + 1) * P, :], waits=w)
            return (b2((i % 2) * 2048, 2048), [(xsem, tgt)])
        norm_transpose(loader, R_HT2, T, TT, None, True, False)
        ht2 = s_ht.v
        rtb = S("rtb")
        dma("sync", rtb, b2(0, 2048), gnw_d[:, :], waits=[(s_ht, ht2)])
        dma("sync", rtb, b2(2048, 2048), gnb_d[:, :])
        dma("sync", rtb, b2(4096, 4096), sin_d[:, :])
        t_cos = dma("sync", rtb, b2(8192, 4096), cos_d[:, :])
        tbl = [(rtb, t_cos)]
        s_gp = S("s_gp")
        gp_stg = {}
        rw_rel = [None]

        rw = S("rw")

        def load_mats(mats, h):
            tg = None
            for mi, mm_ in enumerate(mats):
                ap = a_w[mm_].rearrange("(c p) n -> p c n", p=P)[
                    :, :, h * HDR:(h + 1) * HDR]
                sb = b1(R_RW + mi * 4096, 4096).rearrange(
                    "p (c n) -> p c n", c=DC)
                if rw_rel[0] is not None:
                    w = [rw_rel[0]]
                else:
                    w = [(s_ht, ht2), (S("edsp"), S("edsp").v)]
                tg = dma("gpsimd", rw, sb, ap, waits=w)
            return tg

        nhr_lim = RET_HEADS
        bias_mi = {"q": 0, "k": 1, "v": 2, "g": 3}
        rbia = S("rbia")

        def load_bias(h):
            tg = None
            for mm_, mi in bias_mi.items():
                if rw_rel[0] is not None:
                    w = [rw_rel[0]]
                else:
                    w = [(s_ht, ht2)] + [(S(f"pxs{i}"), S(f"pxs{i}").v)
                                         for i in range(3)]
                tg = dma("gpsimd", rbia,
                         B1[0:1, R_BIAS + mi * HDR:R_BIAS + (mi + 1) * HDR],
                         a_b[mm_][:, h * HDR:(h + 1) * HDR], waits=w)
            return tg

        t_qk = load_mats("qk", 0)
        t_bias = load_bias(0)
        TMP1, TMP2, RBUF = 12288, 14336, 12288
        for h in range(nhr_lim):
            for mi, mm_ in enumerate("qk"):
                dstb = R_QF if mm_ == "q" else R_KF
                gtab = gqs if mm_ == "q" else gks
                bo = R_BIAS + bias_mi[mm_] * HDR
                sinb, cosb = 4096, 8192
                for half in range(2):
                    for tl in range(TTH):
                        tt = half * TTH + tl
                        m = M.next()
                        w = m.begin() + [(rw, t_qk), (s_ht, ht2), (rbia, t_bias)]
                        for c in range(DC):
                            E("tensor", "matmul", m.ps[:, 0:HDR],
                              b1(R_HT2 + c * T + tt * P, P),
                              b1(R_RW + mi * 4096 + c * HDR, HDR),
                              start=(c == 0), stop=False,
                              waits=(w if c == 0 else ()))
                        E("tensor", "matmul", m.ps[:, 0:HDR],
                          ones[0:1, 0:P], B1[0:1, bo:bo + HDR],
                          start=False, stop=True, inc=m.done)
                        E("vector", "tensor_copy", b2(TMP1 + tl * HDR, HDR),
                          m.ps[:, 0:HDR], waits=m.rd(), inc=m.free)
                    t1v = b2(TMP1, 2048)
                    t2v = b2(TMP2, 2048)
                    E("vector", "tensor_copy", evens(t2v), odds(t1v),
                      waits=tbl)
                    E("vector", "tensor_copy", odds(t2v), evens(t1v))
                    E("vector", "tensor_tensor", t2v, t2v,
                      b2(sinb + half * 2048, 2048), op=ALU.mult)
                    E("vector", "tensor_tensor", t1v, t1v,
                      b2(cosb + half * 2048, 2048), op=ALU.mult)
                    E("vector", "tensor_tensor", t1v, t1v, t2v, op=ALU.add)
                    gb = bass.AP(gtab, h * TT + half * TTH,
                                 [[NHR * TT, P], [1, TTH], [0, HDR]])
                    w5 = [(S("gts0"), S("gts0").v), (S("gts1"), S("gts1").v)]
                    if PE() is not None:
                        w5.append(PE())
                    E("vector", "tensor_tensor",
                      B5[:, :].rearrange("p (s n) -> p s n", s=TTH),
                      t1v.rearrange("p (s n) -> p s n", s=TTH), gb,
                      op=ALU.mult, waits=w5, inc=s_ev)
                    rotd = s_ev.v
                    for g in range(2):
                        ts = TSR.next()
                        w2 = ts.begin() + ([(s_ev, rotd)] if g == 0 else [])
                        for j in range(8):
                            E("tensor", "transpose",
                              ts.ps[:, j * P:(j + 1) * P],
                              B5[:, (g * 8 + j) * P:(g * 8 + j + 1) * P],
                              ident[:, :], waits=(w2 if j == 0 else ()),
                              inc=(ts.done if j == 7 else None))
                        for j in range(8):
                            tl = (g * 8 + j) // 2
                            cc = (g * 8 + j) % 2
                            dcol = dstb + cc * T + (half * TTH + tl) * P
                            E("vector", "tensor_copy", b1(dcol, P),
                              ts.ps[:, j * P:(j + 1) * P],
                              waits=(ts.rd() if j == 0 else ()),
                              inc=(ts.free if j == 7 else None))
            rw_rel[0] = PE()
            if RET_LVL < 2:
                continue
            t_vg = load_mats("vg", h)
            for mi, mm_ in enumerate("vg"):
                dstb = R_VT if mm_ == "v" else R_GR
                bo = R_BIAS + bias_mi[mm_] * HDR
                for tt in range(TT):
                    m = M.next()
                    w = m.begin() + [(rw, t_vg)]
                    if tt == 0 and mi == 0:
                        w += [(s_ac, s_ac.v)]
                    for c in range(DC):
                        E("tensor", "matmul", m.ps[:, 0:HDR],
                          b1(R_HT2 + c * T + tt * P, P),
                          b1(R_RW + mi * 4096 + c * HDR, HDR),
                          start=(c == 0), stop=False,
                          waits=(w if c == 0 else ()))
                    E("tensor", "matmul", m.ps[:, 0:HDR],
                      ones[0:1, 0:P], B1[0:1, bo:bo + HDR],
                      start=False, stop=True, inc=m.done)
                    E("vector", "tensor_copy", b1(dstb + tt * HDR, HDR),
                      m.ps[:, 0:HDR], waits=m.rd(), inc=m.free)
            rw_rel[0] = PE()
            E("vector", "engine_nop", inc=s_ev)
            vg_done = s_ev.v
            if h < nhr_lim - 1:
                t_qk = load_mats("qk", h + 1)
                t_bias = load_bias(h + 1)
            if RET_LVL < 3:
                continue
            for qt in range(TT):
                nkt = qt + 1
                for kt in range(nkt):
                    m = M.next()
                    w = m.begin() + ([(s_ev, vg_done)] if kt == 0 else [])
                    for cc in range(2):
                        E("tensor", "matmul", m.ps[:, 0:P],
                          b1(R_KF + cc * T + kt * P, P),
                          b1(R_QF + cc * T + qt * P, P),
                          start=(cc == 0), stop=(cc == 1),
                          waits=(w if cc == 0 else ()),
                          inc=(m.done if cc == 1 else None))
                    wse = list(m.rd())
                    if h == 0 and qt == 0:
                        wse += [(S("pxs1"), S("pxs1").v),
                                (S("pxs2"), S("pxs2").v)]
                    if kt == qt:
                        E("vector", "scalar_tensor_tensor",
                          b1(R_SIMT + kt * P, P), m.ps[:, 0:P], 1.0,
                          maskR[:, :], op0=ALU.mult, op1=ALU.mult,
                          waits=wse, inc=m.free)
                    else:
                        E("vector", "tensor_copy", b1(R_SIMT + kt * P, P),
                          m.ps[:, 0:P], waits=wse, inc=m.free)
                E("vector", "engine_nop", inc=s_ev)
                simd = s_ev.v
                if RET_LVL < 4:
                    continue
                m = M.next()
                w = m.begin() + [(s_ev, simd)]
                for kt in range(nkt):
                    E("tensor", "matmul", m.ps[:, 0:HDR],
                      b1(R_SIMT + kt * P, P), b1(R_VT + kt * HDR, HDR),
                      start=(kt == 0), stop=(kt == nkt - 1),
                      waits=(w if kt == 0 else ()),
                      inc=(m.done if kt == nkt - 1 else None))
                E("vector", "tensor_copy", b2(RBUF + qt * HDR, HDR),
                  m.ps[:, 0:HDR], waits=m.rd(), inc=m.free)
                E("vector", "reduce_sum", ST[:, qt:qt + 1],
                  b2(RBUF + qt * HDR, HDR), axis=AX.X)
                E("vector", "scalar_tensor_tensor", B6[:, 0:HDR],
                  b2(RBUF + qt * HDR, HDR), 1.0, b2(RBUF + qt * HDR, HDR),
                  op0=ALU.mult, op1=ALU.mult,
                  accum_out=ST[:, 16 + qt:17 + qt])
            E("vector", "tensor_scalar", ST[:, 32:48], ST[:, 0:16],
              1.0 / HDR, None, op0=ALU.mult)
            E("vector", "tensor_scalar", ST[:, 16:32], ST[:, 16:32],
              1.0 / HDR, None, op0=ALU.mult)
            E("vector", "tensor_tensor", ST[:, 0:16], ST[:, 32:48],
              ST[:, 32:48], op=ALU.mult)
            E("vector", "tensor_tensor", ST[:, 16:32], ST[:, 16:32],
              ST[:, 0:16], op=ALU.subtract, inc=s_ev)
            E("scalar", "activation", ST[:, 0:16], ST[:, 16:32], AF.Ln,
              bias=epsc[:, 0:1], waits=[(s_ev, s_ev.v)])
            E("scalar", "activation", ST[:, 48:64], ST[:, 0:16], AF.Exp,
              scale=-0.5, inc=s_ac)
            rb3 = b2(RBUF, 4096).rearrange("p (s n) -> p s n", s=TT)
            mb = bass.AP(ST, 32, [[64, P], [1, TT], [0, HDR]])
            rsb = bass.AP(ST, 48, [[64, P], [1, TT], [0, HDR]])
            gnwb = bass.AP(B2, h * HDR, [[16384, P], [0, TT], [1, HDR]])
            gnbb = bass.AP(B2, 2048 + h * HDR, [[16384, P], [0, TT], [1, HDR]])
            E("vector", "tensor_tensor", rb3, rb3, mb, op=ALU.subtract,
              waits=[(s_ac, s_ac.v)])
            E("vector", "tensor_tensor", rb3, rb3, rsb, op=ALU.mult)
            E("vector", "tensor_tensor", rb3, rb3, gnwb, op=ALU.mult)
            E("vector", "tensor_tensor", rb3, rb3, gnbb, op=ALU.add, inc=s_ev)
            yd = s_ev.v
            for half in range(2):
                gsem = S(f"gts{half}")
                gprev = S(f"gts{1 - half}")
                E("scalar", "activation", B5[:, :],
                  b1(R_GR + half * 2048, 2048), AF.Silu,
                  waits=[(s_ev, yd), PE(), (gprev, gprev.v)], inc=s_ac)
                E("vector", "tensor_tensor", B5[:, :],
                  b2(RBUF + half * 2048, 2048), B5[:, :], op=ALU.mult,
                  waits=[(s_ac, s_ac.v)], inc=s_gp)
                dma("sync", gsem, sp_gate[h, :, half * 2048:(half + 1) * 2048],
                    B5[:, :], waits=[(s_gp, s_gp.v)])
        if not RET_WO2:
            return
        bo2 = S("bo2")
        t_bo = dma("gpsimd", bo2, B1[0:1, G_BO:G_BO + D], a_b["o"][:, :],
                   waits=[PE(), (s_ac, s_ac.v), (s_gp, s_gp.v)])
        wo2c = S("wo2c")
        for c in range(DC):
            wo2_full = dma("gpsimd", wo2c, b1(G_WOCH + c * 2048, 2048),
                           a_w["o"][c * P:(c + 1) * P, :],
                           waits=([PE()] if c == 0 else []),
                           max_dma_last_dim=8192)
        s_gt = S("s_gt")
        s_x1 = S("s_x1")
        x1_done = {}
        t3_tgt = {}
        for thalf in range(2):
            gt_nops = {}
            for h in range(NHR):
                gslot = G_SRC + (h % 2) * 2048
                gls = S(f"gsl{h % 2}")
                w = [gt_nops[h - 2]] if h >= 2 else                     [(S("gts0"), S("gts0").v), (S("gts1"), S("gts1").v)]
                tgl = dma("sync", gls, b1(gslot, 2048),
                          sp_gate[h, :, thalf * 2048:(thalf + 1) * 2048],
                          waits=w)
                for g in range(2):
                    ts = TSR.next()
                    w2 = ts.begin() + ([(gls, tgl)] if g == 0 else [])
                    for j in range(8):
                        tl = g * 8 + j
                        tloc = tl // 2
                        cc = tl % 2
                        E("tensor", "transpose", ts.ps[:, j * P:(j + 1) * P],
                          b1(gslot + tloc * HDR + cc * P, P),
                          ident[:, :], waits=(w2 if j == 0 else ()),
                          inc=(ts.done if j == 7 else None))
                    for j in range(8):
                        tl = g * 8 + j
                        tloc = tl // 2
                        cc = tl % 2
                        dcol = G_GT + (2 * h + cc) * TH + tloc * P
                        E("vector", "tensor_copy", b1(dcol, P),
                          ts.ps[:, j * P:(j + 1) * P],
                          waits=(ts.rd() if j == 0 else ()),
                          inc=(ts.free if j == 7 else None))
                gt_nops[h] = PE()
            E("vector", "engine_nop", inc=s_ev)
            gt_done = s_ev.v
            for tl in range(TTH):
                tt = thalf * TTH + tl
                xls = S(f"x1l{tt % 2}")
                if (tt % 2) in x1_done and x1_done[tt % 2] is not None:
                    w0 = [(s_ev, x1_done[tt % 2])]
                else:
                    w0 = [(s_ev, s_ev.v), (s_ac, s_ac.v)]
                t_x1 = dma("scalar", xls, b2((tt % 2) * 2048, 2048),
                           sp_xA[tt * P:(tt + 1) * P, :], waits=w0)
                for dh in range(2):
                    m = M.next()
                    w = m.begin() + [(s_ev, gt_done), (bo2, t_bo)]
                    if tt == 0:
                        w += [(wo2c, wo2_full)]
                    for c in range(DC):
                        for nn in range(2):
                            E("tensor", "matmul",
                              m.ps[:, nn * 512:(nn + 1) * 512],
                              b1(G_GT + c * TH + tl * P, P),
                              b1(G_WOCH + c * 2048 + dh * 1024 + nn * 512, 512),
                              start=(c == 0), stop=False,
                              waits=(w if (c == 0 and nn == 0) else ()))
                    for nn in range(2):
                        E("tensor", "matmul", m.ps[:, nn * 512:(nn + 1) * 512],
                          ones[0:1, 0:P],
                          B1[0:1, G_BO + dh * 1024 + nn * 512:
                             G_BO + dh * 1024 + nn * 512 + 512],
                          start=False, stop=True,
                          inc=(m.done if nn == 1 else None))
                    ww = list(m.rd()) + [(xls, t_x1)]
                    if dh == 0 and (4096 in t3_tgt):
                        ww.append(t3_tgt[4096])
                        ww.append((s_ac, s_ac.v))
                    E("vector", "scalar_tensor_tensor",
                      b2(4096 + dh * 1024, 1024), m.ps[:, 0:1024], 1.0,
                      b2((tt % 2) * 2048 + dh * 1024, 1024),
                      op0=ALU.mult, op1=ALU.add, waits=ww, inc=m.free)
                E("vector", "engine_nop", inc=s_ev)
                x2done = s_ev.v
                x1_done[tt % 2] = x2done
                xbs_t = dma("sync", S("xbs"), sp_xB[tt * P:(tt + 1) * P, :],
                            b2(4096, 2048), waits=[(s_ev, x2done)])
                xsrc = b2(4096, 2048)
                E("vector", "reduce_sum", ST[:, 0:1], xsrc, axis=AX.X,
                  waits=[(s_ev, x2done)], inc=None)
                E("scalar", "activation", B5[:, :], xsrc, AF.Square,
                  accum_out=ST[:, 1:2], waits=[(s_ev, x2done)], inc=None)
                stat_chain(ST[:, 0:1], ST[:, 1:2], ST[:, 2:3], ST[:, 7:8],
                           EPS_LN, D)
                w4 = [(s_ac, s_ac.v)]
                if 6144 in t3_tgt:
                    w4.append(t3_tgt[6144])
                E("vector", "tensor_scalar", b2(6144, 2048), xsrc, ST[:, 2:3],
                  ST[:, 7:8], op0=ALU.subtract, op1=ALU.mult, waits=w4,
                  inc=s_ev)
                t3_tgt[6144] = (S("t3s"), dma(
                    "sync", S("t3s"), sp_t[tt * P:(tt + 1) * P, :],
                    b2(6144, 2048), waits=[(s_ev, s_ev.v)]))
                t3_tgt[4096] = (S("xbs"), xbs_t)


    def dbg_copy(src_dram, cast=False):
        q = S("dbgq")
        eng = "gpsimd" if cast else "sync"
        wl = [(s_ev, s_ev.v), (s_ac, s_ac.v)]
        for nm, sm in list(kb._sems.items()):
            if nm not in ("dbgq",):
                wl.append((sm, sm.v))
        t = dma(eng, q, dbg[:, :], src_dram, waits=wl)
        W("sync", wl + [(q, t)])

    ph_consts()
    order = ["ln1", "attn1", "wo1", "h2", "cw", "hh", "moe1", "retnet", "full"]
    stop = order.index(upto) if upto in order else len(order) - 1
    if upto == "h2":
        stop = 2
    if upto == "cw":
        stop = 5
    if upto == "hh":
        stop = 5
    ph_norm1(0)
    if upto == "ln1":
        dbg_copy(sp_t[:, :])
    if stop >= 1:
        ph_attn(0)
        if upto == "attn1":
            dbg_copy(sp_ao.rearrange("h p q -> (h p) q"), cast=True)
    if stop >= 2:
        ph_wo(0)
        if upto == "h2":
            t_h2 = kb.E("gpsimd", "dma_start",
                        dbg.rearrange("(a p) d -> p a d", p=P)[:, 0:8, :],
                        B1[:, 0:16384].rearrange("p (a d) -> p a d", a=8),
                        waits=[(s_ht, s_ht.v)], inc=S("dbgq"), incn=16)
            W("sync", [(S("dbgq"), t_h2)])
        if upto == "wo1":
            for i in range(TTH):
                dma("sync", S("dbg2"), sp_x2[i * P:(i + 1) * P, :],
                    b2(i * 2048, 2048), waits=[(s_ht, s_ht.v)])
            dbg_copy(sp_x2[:, :])
    if stop >= 4:
        ph_moe_half(0, 0)
        if upto not in ("cw", "hh"):
            ph_moe_half(0, 1)
        if upto == "moe1":
            dbg_copy(sp_xA[:, :])
    if stop >= 7:
        ph_retnet()
        if upto == "retnet":
            dbg_copy(sp_xB[:, :])
    if stop >= 8:
        ph_norm1(1)
        ph_attn(1)
        ph_wo(1)
        ph_moe_half(1, 0)
        ph_moe_half(1, 1)

    W("sync", [(sm, sm.v) for sm in kb._sems.values()])
    W("vector", [(s_ev, s_ev.v)])
    W("scalar", [(s_ac, s_ac.v)])

    with nc.Block() as block:
        kb.dispatch(block)
    ctx.close()
    return nc, kb.n_instr


def prep_inputs(inputs):
    gi = {k: np.asarray(v) for k, v in inputs.items()}
    c = host_consts()
    base = {
        "a_wq": gi["L0_attn_w_q"], "a_wk": gi["L0_attn_w_k"],
        "a_wv": gi["L0_attn_w_v"], "a_wg": gi["L0_attn_w_g"],
        "a_wo": gi["L0_attn_w_o"],
        "a_bq": gi["L0_attn_b_q"].reshape(1, D),
        "a_bk": gi["L0_attn_b_k"].reshape(1, D),
        "a_bv": gi["L0_attn_b_v"].reshape(1, D),
        "a_bg": gi["L0_attn_b_g"].reshape(1, D),
        "a_bo": gi["L0_attn_b_o"].reshape(1, D),
        "gnw": np.broadcast_to(gi["L0_attn_gn_weight"][None, :], (P, D)),
        "gnb": np.broadcast_to(gi["L0_attn_gn_bias"][None, :], (P, D)),
        "f_ln1": gi["L0_ffn_ln1_w"].reshape(DC, P).T,
        "f_ln2": gi["L0_ffn_ln2_w"].reshape(DC, P).T,
        "f_wq": gi["L0_ffn_wq"], "f_wk": gi["L0_ffn_wk"],
        "f_wv": gi["L0_ffn_wv"], "f_wo": gi["L0_ffn_wo"],
        "f_gate": gi["L0_ffn_gate_w"],
        "f_eg": gi["L0_ffn_eg_w"], "f_eu": gi["L0_ffn_eu_w"],
        "f_ed": gi["L0_ffn_ed_w"],
        "f_sg": gi["L0_ffn_sg_w"], "f_su": gi["L0_ffn_su_w"],
        "f_sd": gi["L0_ffn_sd_w"],
        "ident": c["ident"], "identF": c["identF"], "epsc": c["epsc"], "maskS": c["maskS"], "maskR": c["maskR"],
        "ones_bf": c["ones_bf"], "gq": c["gq"], "gk": c["gk"],
        "sin_sg": c["sin_sg"], "cos_t": c["cos_t"],
    }
    out = {}
    for k, v in base.items():
        if v.dtype == ml_dtypes.bfloat16:
            out[k] = np.ascontiguousarray(v)
        else:
            out[k] = np.ascontiguousarray(v, dtype=np.float32)
    x = np.asarray(gi["L0_attn_x"], np.float32)
    in_maps = []
    for core in range(8):
        m = dict(out)
        m["x"] = np.ascontiguousarray(x[core])
        in_maps.append(m)
    return in_maps


_CACHE = {}


def _get_prog(upto="full"):
    if upto not in _CACHE:
        _CACHE[upto] = build_program(upto)
    return _CACHE[upto]


def kernel(**inputs):
    nc, n = _get_prog("full")
    in_maps = prep_inputs(inputs)
    res = run_bass_kernel_spmd(nc, in_maps, core_ids=list(range(8)))
    return np.stack([np.asarray(r["y"], np.float32) for r in res.results],
                    axis=0)
